# version 32
# baseline (speedup 1.0000x reference)
import math
import contextlib
import numpy as np
import ml_dtypes
import concourse.bass as bass
import concourse.mybir as mybir
import concourse.bass_utils as _bu
from concourse.bass_utils import run_bass_kernel_spmd

if not hasattr(_bu, "_orig_run_command"):
    _bu._orig_run_command = _bu.run_command


def _run_command_ldwopt(argv, **kwargs):
    argv = ["--enable-ldw-opt=true" if a == "--enable-ldw-opt=false" else a
            for a in argv]
    return _bu._orig_run_command(argv, **kwargs)


_bu.run_command = _run_command_ldwopt

F32 = mybir.dt.float32
BF16 = mybir.dt.bfloat16
AF = mybir.ActivationFunctionType
ALU = mybir.AluOpType
AX = mybir.AxisListType

T = 2048
D = 2048
NT = 16
ND = 16
H = 16
GK = 4
DH = 128
HF = 8192
NA = 64
INV_SQRT_DH = 1.0 / math.sqrt(DH)
FWHT_SCALE = 1.0 / math.sqrt(HF)
NEG = -1e30
BF = ml_dtypes.bfloat16


def _consts():
    i = np.arange(128)
    pop = np.zeros((128, 128), np.int64)
    for r in range(128):
        pop[r] = [bin(r & j).count("1") & 1 for j in i]
    had = (1.0 - 2.0 * pop).astype(np.float32)
    ident = np.eye(128, dtype=np.float32)
    inv_freq = 10000.0 ** (-np.arange(0, 128, 2, np.float64) / 128.0)
    fr = np.outer(np.arange(T, dtype=np.float64), inv_freq)
    cosl = np.cos(fr).astype(np.float32).reshape(NT, 128, 64).transpose(
        1, 0, 2).reshape(128, NT * 64).copy()
    sinl = np.sin(fr).astype(np.float32).reshape(NT, 128, 64).transpose(
        1, 0, 2).reshape(128, NT * 64).copy()
    logp = (np.log(np.arange(1, T + 1, dtype=np.float64)) / np.sqrt(128.0)
            ).astype(np.float32).reshape(NT, 128).T.copy()
    ii = np.arange(128)[:, None]
    jj = np.arange(128)[None, :]
    triu = np.where(jj >= ii, 0.0, NEG).astype(np.float32)
    tril = np.where(jj <= ii, 0.0, NEG).astype(np.float32)
    ceps = np.zeros((128, 8), np.float32)
    ceps[:, 0] = 1e-6
    ceps[:, 1] = 1e-5
    ceps[:, 2] = 1.0
    return dict(had=had.astype(BF), ident=ident.astype(BF), cos=cosl, sin=sinl,
                logp=logp, triu=triu.astype(BF), tril=tril.astype(BF),
                ceps=ceps)


class Sem:
    def __init__(self, ctx, nc, name):
        self.h = ctx.enter_context(nc.semaphore(name))
        self.n = 0
        self.name = name


class Em:

    ENGINES = ("tensor", "vector", "scalar", "sync", "gpsimd")

    def __init__(self, ctx, nc):
        self.nc = nc
        self.ctx = ctx
        self.ops = {e: [] for e in self.ENGINES}
        self.prog = {e: Sem(ctx, nc, f"pg_{e}") for e in self.ENGINES}
        self.res = {}
        self.cover = {e: {} for e in self.ENGINES}
        self.sems = {}
        self.nwaits = 0

    def stream_sem(self, name):
        if name not in self.sems:
            self.sems[name] = Sem(self.ctx, self.nc, name)
        return self.sems[name]

    def _handle(self, name):
        if name in self.sems:
            return self.sems[name].h
        for e in self.ENGINES:
            if self.prog[e].name == name:
                return self.prog[e].h
        raise KeyError(name)

    def _needs(self, eng, reads, writes):
        need = {}

        def add(sn, v):
            if self.cover[eng].get(sn, -1) >= v:
                return
            if need.get(sn, -1) < v:
                need[sn] = v

        for k in reads:
            st = self.res.get(k)
            if st:
                for sn, v in st["w"].items():
                    add(sn, v)
        for k in writes:
            st = self.res.get(k)
            if st:
                for sn, v in st["w"].items():
                    add(sn, v)
                for sn, v in st["r"].items():
                    add(sn, v)
        return need

    def _commit(self, eng, need, semname, semval, reads, writes):
        for sn, v in need.items():
            if self.cover[eng].get(sn, -1) < v:
                self.cover[eng][sn] = v
        for k in writes:
            self.res[k] = {"w": {semname: semval}, "r": {}}
        for k in reads:
            st = self.res.setdefault(k, {"w": {}, "r": {}})
            if st["r"].get(semname, -1) < semval:
                st["r"][semname] = semval

    def _mk_emit(self, fns, need):
        handles = {n: self._handle(n) for n in need}
        self.nwaits += len(need)

        def emit(e, fns=fns, need=dict(need), handles=handles):
            for n, v in need.items():
                e.wait_ge(handles[n], v)
            last = None
            for f in fns:
                last = f(e)
            return last

        return emit

    def op(self, eng, fn, reads=(), writes=()):
        self.group(eng, [fn], reads, writes)

    def group(self, eng, fns, reads=(), writes=()):
        need = self._needs(eng, reads, writes)
        sem = self.prog[eng]
        sem.n += 1
        inner = self._mk_emit(fns, need)

        def emit(e, inner=inner, sem=sem):
            inner(e).then_inc(sem.h, 1)

        self.ops[eng].append(emit)
        self._commit(eng, need, sem.name, sem.n, reads, writes)

    def dma(self, eng, stream, out, in_, reads=(), writes=(), transpose=False):
        ssem = self.stream_sem(stream)
        need = self._needs(eng, reads, writes)
        ssem.n += 16
        inner = self._mk_emit(
            [lambda e, out=out, in_=in_, tr=transpose:
             e.dma_start(out, in_, transpose=tr)], need)

        def emit(e, inner=inner, ssem=ssem):
            inner(e).then_inc(ssem.h, 16)

        self.ops[eng].append(emit)
        for sn, v in need.items():
            if self.cover[eng].get(sn, -1) < v:
                self.cover[eng][sn] = v
        for k in writes:
            self.res[k] = {"w": {stream: ssem.n}, "r": {}}
        for k in reads:
            st = self.res.setdefault(k, {"w": {}, "r": {}})
            if st["r"].get(stream, -1) < ssem.n:
                st["r"][stream] = ssem.n

    def wait(self, eng, semname, val):
        h = self._handle(semname)
        self.ops[eng].append(lambda e, h=h, val=val: e.wait_ge(h, val))
        if self.cover[eng].get(semname, -1) < val:
            self.cover[eng][semname] = val

    def overlay(self, newkey, oldkeys):
        st = {"w": {}, "r": {}}
        for k in oldkeys:
            o = self.res.pop(k, None)
            if not o:
                continue
            for sn, v in o["w"].items():
                st["r"][sn] = max(st["r"].get(sn, -1), v)
            for sn, v in o["r"].items():
                st["r"][sn] = max(st["r"].get(sn, -1), v)
        self.res[newkey] = st

    def barrier(self, extra_streams=()):
        targets = [(self.prog[e].name, self.prog[e].n)
                   for e in ("tensor", "vector", "scalar")]
        targets += [(s, self.sems[s].n) for s in extra_streams if s in self.sems]
        for e in self.ENGINES:
            for sn, v in targets:
                if v > 0 and self.cover[e].get(sn, -1) < v:
                    self.wait(e, sn, v)
        self.res = {}


def build_program(dbg=()):
    C = _consts()
    nc = bass.Bass("TRN2", target_bir_lowering=False)

    x_in = nc.dram_tensor("x", [T, D], F32, kind="ExternalInput").ap()
    wq_in = nc.dram_tensor("wq", [D, H * DH], F32, kind="ExternalInput").ap()
    wk_in = nc.dram_tensor("wk", [D, GK * DH], F32, kind="ExternalInput").ap()
    wv_in = nc.dram_tensor("wv", [D, GK * DH], F32, kind="ExternalInput").ap()
    wg_in = nc.dram_tensor("wg", [D, H * DH], F32, kind="ExternalInput").ap()
    scal_in = nc.dram_tensor("scaler_rep", [128, H], F32, kind="ExternalInput").ap()
    gn_in = nc.dram_tensor("gn_rep", [128, DH], F32, kind="ExternalInput").ap()
    w1_in = nc.dram_tensor("w1", [HF, D], F32, kind="ExternalInput").ap()
    w2_in = nc.dram_tensor("w2", [D, HF], F32, kind="ExternalInput").ap()
    w3_in = nc.dram_tensor("w3", [HF, D], F32, kind="ExternalInput").ap()
    out_d = nc.dram_tensor("out", [T, D], F32, kind="ExternalOutput").ap()

    s_scr = nc.dram_tensor("s_scr", [NT * 128, D], BF16, kind="Internal").ap()
    hid_scr = nc.dram_tensor("hid_scr", [4, NA, 128, 512], BF16,
                             kind="Internal").ap()
    w2pp = nc.dram_tensor("w2pp", [8, 128, 2 * HF], BF16, kind="Internal").ap()

    cin = {}
    for nm, arr in (("ident", C["ident"]), ("had", C["had"]), ("cos", C["cos"]),
                    ("sin", C["sin"]), ("logp", C["logp"]), ("triu", C["triu"]),
                    ("tril", C["tril"]), ("ceps", C["ceps"])):
        cin[nm] = nc.inline_tensor(arr, name=f"c_{nm}").ap()

    dbg_outs = {}

    ctx = contextlib.ExitStack()
    with ctx:
        em = Em(ctx, nc)

        def sball(name, cols, dtype, offset):
            return nc.alloc_sbuf_tensor_at(name, [128, cols], dtype,
                                           offset=offset)

        off = int(nc._sbuf_addr_for_side("left"))

        def nxt(cols, dtype, align=64):
            nonlocal off
            esz = 2 if dtype == BF16 else 4
            off = (off + align - 1) // align * align
            o = off
            off += cols * esz
            return o

        IDENT = sball("IDENT", 128, BF16, nxt(128, BF16))
        HAD = sball("HAD", 128, BF16, nxt(128, BF16))
        COS = sball("COS", NT * 64, F32, nxt(NT * 64, F32))
        SIN = sball("SIN", NT * 64, F32, nxt(NT * 64, F32))
        LOGP = sball("LOGP", NT, F32, nxt(NT, F32))
        TRIU = sball("TRIU", 128, BF16, nxt(128, BF16))
        TRIL = sball("TRIL", 128, BF16, nxt(128, BF16))
        CEPS = sball("CEPS", 8, F32, nxt(8, F32))
        SCAL = sball("SCAL", H, F32, nxt(H, F32))
        GN = sball("GN", DH, F32, nxt(DH, F32))
        ST = sball("ST", 128, F32, nxt(128, F32))
        xnT_off = nxt(ND * T, BF16)
        xnT = sball("xnT", ND * T, BF16, xnT_off)
        arena0 = off

        offA = arena0

        def nxtA(cols, dtype, align=64):
            nonlocal offA
            esz = 2 if dtype == BF16 else 4
            offA = (offA + align - 1) // align * align
            o = offA
            offA += cols * esz
            return o

        kT = sball("kT", GK * T, BF16, nxtA(GK * T, BF16))
        v_st = sball("v_st", NT * 512, BF16, nxtA(NT * 512, BF16))
        wq_bf = sball("wq_bf", ND * 512, BF16, nxtA(ND * 512, BF16))
        qT = sball("qT", 4 * T, BF16, nxtA(4 * T, BF16))
        gsil = sball("gsil", NT * 512, BF16, nxtA(NT * 512, BF16))
        p_bf = sball("p_bf", 2 * 1280, BF16, nxtA(2 * 1280, BF16))
        pT_bf = sball("pT_bf", 2 * 512, BF16, nxtA(2 * 512, BF16))
        qf = sball("qf", 2 * 512, F32, nxtA(2 * 512, F32))
        sqs = sball("sqs", 2 * 512, F32, nxtA(2 * 512, F32))
        rtmp = sball("rtmp", 512, F32, nxtA(512, F32))
        w_all = sball("w_all", 512, F32, nxtA(512, F32))
        w_all2 = sball("w_all2", 512, F32, nxtA(512, F32))
        sstage = sball("sstage", 2 * 512, BF16, nxtA(2 * 512, BF16))
        kqbf = sball("kqbf", 2 * 512, BF16, nxtA(2 * 512, BF16))
        endA = offA

        xt = sball("xt", 2 * D, F32, arena0)
        ln_scr = sball("ln_scr", D, F32, arena0 + 16384)
        xnbf = sball("xnbf", 2 * D, BF16, arena0 + 24576)

        offB = arena0

        def nxtB(cols, dtype, align=64):
            nonlocal offB
            esz = 2 if dtype == BF16 else 4
            offB = (offB + align - 1) // align * align
            o = offB
            offB += cols * esz
            return o

        w1nat = sball("w1nat", 4 * D, BF16, nxtB(4 * D, BF16))
        w3nat = sball("w3nat", 4 * D, BF16, nxtB(4 * D, BF16))
        w1T = sball("w1T", 2 * D, BF16, nxtB(2 * D, BF16))
        w3T = sball("w3T", 2 * D, BF16, nxtB(2 * D, BF16))
        hidb = sball("hidb", 2 * D, BF16, nxtB(2 * D, BF16))
        sg_scr = sball("sg_scr", 2 * 512, F32, nxtB(2 * 512, F32))
        w2x = sball("w2x", HF, BF16, nxtB(HF, BF16))
        w2y = sball("w2y", HF, BF16, nxtB(HF, BF16))
        w2z = sball("w2z", HF, BF16, nxtB(HF, BF16))
        w2st = sball("w2st", HF, BF16, nxtB(HF, BF16))
        endB = offB

        offC = xnT_off

        def nxtC(cols, dtype, align=64):
            nonlocal offC
            esz = 2 if dtype == BF16 else 4
            offC = (offC + align - 1) // align * align
            o = offC
            offC += cols * esz
            return o

        hidc = sball("hidc", NA * 512, BF16, nxtC(NA * 512, BF16))
        w2c0 = sball("w2c0", 2 * HF, BF16, nxtC(2 * HF, BF16))
        w2c1 = sball("w2c1", 2 * HF, BF16, nxtC(2 * HF, BF16))
        sTb = sball("sTb", ND * 512, BF16, nxtC(ND * 512, BF16))
        outstage = sball("outstage", 4 * D, F32, nxtC(4 * D, F32))
        gfb = sball("gfb", 2 * 512, F32, nxtC(2 * 512, F32))
        wbf = sball("wbf", 2 * 512, BF16, nxtC(2 * 512, BF16))
        endC = offC

        PBIG = ctx.enter_context(nc.psum_tensor("PBIG", [128, 4096], F32))

        class BankView:
            def __init__(self, k):
                self.k = k

            def ap(self):
                return PBIG.ap()[:, 512 * self.k:512 * (self.k + 1)]

        pb = [BankView(i) for i in range(8)]

        for apv, sbuf, key in (
            (cin["ident"], IDENT, "IDENT"), (cin["had"], HAD, "HAD"),
            (cin["cos"], COS, "COS"), (cin["sin"], SIN, "SIN"),
            (cin["logp"], LOGP, "LOGP"), (cin["triu"], TRIU, "TRIU"),
            (cin["tril"], TRIL, "TRIL"), (cin["ceps"], CEPS, "CEPS"),
            (scal_in, SCAL, "SCAL"), (gn_in, GN, "GN"),
        ):
            em.dma("sync", "scst", sbuf.ap()[:, :], apv[:, :], writes=(key,))

        for key in ("IDENT", "HAD", "COS", "SIN", "LOGP", "TRIU", "TRIL",
                    "CEPS", "SCAL", "GN"):
            em.res[key]["w"]["scst"] = em.sems["scst"].n

        EPS6 = CEPS.ap()[:, 0:1]
        EPS5 = CEPS.ap()[:, 1:2]

        def col(c0, w=1):
            return ST.ap()[:, c0:c0 + w]

        def xnT_t(i, ti):
            return xnT.ap()[:, 2048 * i + 128 * ti:2048 * i + 128 * (ti + 1)]

        def xnT_c(i, c):
            return xnT.ap()[:, 2048 * i + 512 * c:2048 * i + 512 * (c + 1)]

        p0_pend = []

        def p0_transpose(ti, p):
            for bt in range(4):
                bk = 4 + bt % 2
                fns = [lambda e, o=pb[bk].ap()[:, 128 * k:128 * (k + 1)],
                       l=xnbf.ap()[:, p * D + 128 * (4 * bt + k):
                                   p * D + 128 * (4 * bt + k + 1)]:
                       e.matmul(o, l, IDENT.ap()[:, :], start=True, stop=True,
                                skip_group_check=True) for k in range(4)]
                em.group("tensor", fns, reads=(f"xnbf{p}", "IDENT"),
                         writes=(f"pb{bk}",))
                dst = xnT.ap()[:, :].rearrange("p (i t) -> p i t", t=T)[
                    :, 4 * bt:4 * (bt + 1), 128 * ti:128 * (ti + 1)]
                em.op("vector", lambda e, d=dst,
                      s=pb[bk].ap()[:, :].rearrange("p (i t) -> p i t",
                                                    t=128):
                      e.tensor_copy(d, s), reads=(f"pb{bk}",),
                      writes=("xnT",))

        for ti in range(NT):
            p = ti % 2
            xtile = xt.ap()[:, p * D:(p + 1) * D]
            kx = f"xt{p}"
            em.dma("sync", f"sx{p}", xtile, x_in[128 * ti:128 * (ti + 1), :],
                   writes=(kx,))
            s0 = 32 * p
            em.op("vector", lambda e, i_=xtile, o=col(s0 + 0):
                  e.reduce_sum(o, i_, axis=AX.X), reads=(kx,),
                  writes=(f"s1_{p}",))
            em.op("scalar", lambda e, i_=xtile, o=ln_scr.ap()[:, :],
                  acc=col(s0 + 1): e.activation(o, i_, AF.Square, accum_out=acc),
                  reads=(kx,), writes=("ln_scr", f"s2_{p}"))
            em.op("scalar", lambda e, i_=col(s0 + 0), o=col(s0 + 2):
                  e.mul(o, i_, 1.0 / D), reads=(f"s1_{p}",), writes=(f"m_{p}",))
            em.op("vector", lambda e, m=col(s0 + 2), o=col(s0 + 3):
                  e.tensor_mul(o, m, m), reads=(f"m_{p}",), writes=(f"msq_{p}",))
            em.op("vector", lambda e, s2=col(s0 + 1), msq=col(s0 + 3),
                  o=col(s0 + 4): e.tensor_scalar(o, s2, 1.0 / D, msq,
                                                 op0=ALU.mult, op1=ALU.subtract),
                  reads=(f"s2_{p}", f"msq_{p}"), writes=(f"var_{p}",))
            em.op("scalar", lambda e, v=col(s0 + 4), o=col(s0 + 5):
                  e.activation(o, v, AF.Ln, bias=EPS5),
                  reads=(f"var_{p}", "CEPS"), writes=(f"sd_{p}",))
            em.op("scalar", lambda e, i_=col(s0 + 5), o=col(s0 + 6):
                  e.activation(o, i_, AF.Exp, scale=-0.5),
                  reads=(f"sd_{p}",), writes=(f"rstd_{p}",))
            em.op("vector", lambda e, m=col(s0 + 2), r=col(s0 + 6),
                  o=col(s0 + 7): e.tensor_mul(o, m, r),
                  reads=(f"m_{p}", f"rstd_{p}"), writes=(f"mrs_{p}",))
            em.op("scalar", lambda e, i_=col(s0 + 7), o=col(s0 + 8):
                  e.mul(o, i_, -1.0), reads=(f"mrs_{p}",), writes=(f"nmrs_{p}",))
            xnb = xnbf.ap()[:, p * D:(p + 1) * D]
            em.op("vector", lambda e, i_=xtile, o=xnb, sc=col(s0 + 6),
                  bi=col(s0 + 8): e.tensor_scalar(o, i_, sc, bi, op0=ALU.mult,
                                                  op1=ALU.add),
                  reads=(kx, f"rstd_{p}", f"nmrs_{p}"), writes=(f"xnbf{p}",))
            if p0_pend:
                p0_pend.pop(0)()
            p0_pend.append(lambda ti=ti, p=p: p0_transpose(ti, p))
            continue
        _unused = None
        for _never in ():
            for bt in range(4):
                bk = 4 + bt % 2
                fns = [lambda e, o=pb[bk].ap()[:, 128 * k:128 * (k + 1)],
                       l=xnbf.ap()[:, p * D + 128 * (4 * bt + k):
                                   p * D + 128 * (4 * bt + k + 1)]:
                       e.matmul(o, l, IDENT.ap()[:, :], start=True, stop=True,
                                skip_group_check=True) for k in range(4)]
                em.group("tensor", fns, reads=(f"xnbf{p}", "IDENT"),
                         writes=(f"pb{bk}",))
                dst = xnT.ap()[:, :].rearrange("p (i t) -> p i t", t=T)[
                    :, 4 * bt:4 * (bt + 1), 128 * ti:128 * (ti + 1)]
                em.op("vector", lambda e, d=dst,
                      s=pb[bk].ap()[:, :].rearrange("p (i t) -> p i t", t=128):
                      e.tensor_copy(d, s), reads=(f"pb{bk}",), writes=("xnT",))

        while p0_pend:
            p0_pend.pop(0)()

        if "xnT" in dbg:
            dxnT = nc.dram_tensor("dbg_xnT", [128, ND * T], BF16,
                                  kind="ExternalOutput").ap()
            em.dma("gpsimd", "sdbg", dxnT[:, :], xnT.ap()[:, :],
                   reads=("xnT",))

        for i in range(ND):
            em.dma("gpsimd", "skv", wq_bf.ap()[:, 512 * i:512 * (i + 1)],
                   wk_in[128 * i:128 * (i + 1), :], writes=(f"wqbf{i}",))
        for i in range(ND):
            em.dma("gpsimd", "skv", gsil.ap()[:, 512 * i:512 * (i + 1)],
                   wv_in[128 * i:128 * (i + 1), :], writes=(f"wvbf{i}",))
        for i in range(ND):
            em.res[f"wqbf{i}"]["w"]["skv"] = em.sems["skv"].n
            em.res[f"wvbf{i}"]["w"]["skv"] = em.sems["skv"].n

        em.barrier(extra_streams=())
        em.res["xnT"] = {"w": {"pg_vector": em.prog["vector"].n}, "r": {}}
        for i in range(ND):
            em.res[f"wqbf{i}"] = {"w": {"skv": 16 * 2 * ND}, "r": {}}
            em.res[f"wvbf{i}"] = {"w": {"skv": 16 * 2 * ND}, "r": {}}
        for key in ("IDENT", "HAD", "COS", "SIN", "LOGP", "TRIU", "TRIL",
                    "CEPS", "SCAL", "GN"):
            em.res[key] = {"w": {"scst": em.sems["scst"].n}, "r": {}}

        def rope_emit(srcv, dstv, ti, rkeys, wkeys):
            cosv = COS.ap()[:, 64 * ti:64 * (ti + 1)].unsqueeze(1).broadcast_to(
                (128, 4, 64))
            sinv = SIN.ap()[:, 64 * ti:64 * (ti + 1)].unsqueeze(1).broadcast_to(
                (128, 4, 64))
            x1 = srcv[:, :, 0:64]
            x2 = srcv[:, :, 64:128]
            y1 = dstv[:, :, 0:64]
            y2 = dstv[:, :, 64:128]
            t1 = rtmp.ap()[:, 0:256].rearrange("p (h e) -> p h e", e=64)
            t2 = rtmp.ap()[:, 256:512].rearrange("p (h e) -> p h e", e=64)
            em.op("vector", lambda e: e.tensor_mul(t1, x1, cosv),
                  reads=rkeys + ("COS",), writes=("rt1",))
            em.op("vector", lambda e: e.tensor_mul(t2, x2, sinv),
                  reads=rkeys + ("SIN",), writes=("rt2",))
            em.op("vector", lambda e: e.tensor_add(y1, t1, t2),
                  reads=("rt1", "rt2"), writes=wkeys)
            em.op("vector", lambda e: e.tensor_mul(t1, x2, cosv),
                  reads=rkeys + ("COS",), writes=("rt1",))
            em.op("vector", lambda e: e.tensor_mul(t2, x1, sinv),
                  reads=rkeys + ("SIN",), writes=("rt2",))
            em.op("vector", lambda e: e.tensor_sub(y2, t1, t2),
                  reads=("rt1", "rt2"), writes=wkeys)

        def proj_group(bank, bkey, ti, rhs_handle, rkeys):
            fns = [lambda e, o=pb[bank].ap()[:, :], l=xnT_t(i, ti),
                   r=rhs_handle.ap()[:, 512 * i:512 * (i + 1)],
                   s=(i == 0), st=(i == ND - 1):
                   e.matmul(o, l, r, start=s, stop=st, skip_group_check=True)
                   for i in range(ND)]
            em.group("tensor", fns, reads=rkeys + ("xnT",), writes=(bkey,))

        def rms4(bank_ap, bkey, p, out_keybase):
            s0 = 32 * p
            for h in range(4):
                em.op("scalar", lambda e,
                      o=sqs.ap()[:, 512 * p + 128 * h:512 * p + 128 * (h + 1)],
                      i_=bank_ap[:, 128 * h:128 * (h + 1)],
                      acc=col(s0 + 9 + h):
                      e.activation(o, i_, AF.Square, accum_out=acc),
                      reads=(bkey,), writes=(f"sqs{p}", f"kss{p}"))
            em.op("scalar", lambda e, i_=col(s0 + 9, 4), o=col(s0 + 13, 4):
                  e.activation(o, i_, AF.Ln, scale=1.0 / DH, bias=EPS6),
                  reads=(f"kss{p}", "CEPS"), writes=(f"ksd{p}",))
            em.op("scalar", lambda e, i_=col(s0 + 13, 4), o=col(s0 + 17, 4):
                  e.activation(o, i_, AF.Exp, scale=-0.5),
                  reads=(f"ksd{p}",), writes=(out_keybase,))

        def emit_kq_transpose(ti, dst_handle, dst_key):
            p = ti % 2
            tb = 4 + p
            fns = [lambda e, o=pb[tb].ap()[:, 128 * h:128 * (h + 1)],
                   l=kqbf.ap()[:, 512 * p + 128 * h:512 * p + 128 * (h + 1)]:
                   e.matmul(o, l, IDENT.ap()[:, :], start=True, stop=True,
                            skip_group_check=True) for h in range(4)]
            em.group("tensor", fns, reads=(f"kqbf{p}", "IDENT"),
                     writes=(f"pb{tb}",))
            dst = dst_handle.ap()[:, :].rearrange("p (g t) -> p g t", t=T)[
                :, :, 128 * ti:128 * (ti + 1)]
            em.op("vector", lambda e, d=dst,
                  s=pb[tb].ap()[:, :].rearrange("p (g t) -> p g t", t=128):
                  e.tensor_copy(d, s), reads=(f"pb{tb}",), writes=(dst_key,))

        pend_t = []
        for ti in range(NT):
            p = ti % 2
            kb, kkey = p, f"pb{p}"
            vb, vkey = 2 + p, f"pb{2 + p}"
            proj_group(kb, kkey, ti, wq_bf, tuple(f"wqbf{i}" for i in range(ND)))
            proj_group(vb, vkey, ti, gsil, tuple(f"wvbf{i}" for i in range(ND)))
            if pend_t:
                pend_t.pop(0)()
            s0 = 32 * p
            rms4(pb[kb].ap(), kkey, p, f"krs{p}")
            kn = qf.ap()[:, 512 * p:512 * (p + 1)].rearrange(
                "p (h e) -> p h e", e=128)
            em.op("vector", lambda e, o=kn,
                  i_=pb[kb].ap()[:, :].rearrange("p (h e) -> p h e", e=128),
                  r=col(s0 + 17, 4).unsqueeze(2).broadcast_to((128, 4, 128)):
                  e.tensor_tensor(o, i_, r, op=ALU.mult),
                  reads=(kkey, f"krs{p}"), writes=(f"qf{p}",))
            kro = sqs.ap()[:, 512 * p:512 * (p + 1)].rearrange(
                "p (h e) -> p h e", e=128)
            rope_emit(kn, kro, ti, (f"qf{p}",), (f"sqs{p}",))
            em.op("scalar", lambda e, o=kqbf.ap()[:, 512 * p:512 * (p + 1)],
                  i_=sqs.ap()[:, 512 * p:512 * (p + 1)]: e.copy(o, i_),
                  reads=(f"sqs{p}",), writes=(f"kqbf{p}",))
            pend_t.append(lambda ti=ti: emit_kq_transpose(ti, kT, "kT"))
            em.op("scalar", lambda e, o=v_st.ap()[:, 512 * ti:512 * (ti + 1)],
                  i_=pb[vb].ap()[:, :]: e.copy(o, i_),
                  reads=(vkey,), writes=("v_st",))
        while pend_t:
            pend_t.pop(0)()

        if "kv" in dbg:
            dkT = nc.dram_tensor("dbg_kT", [128, GK * T], BF16,
                                 kind="ExternalOutput").ap()
            em.dma("gpsimd", "sdbg", dkT[:, :], kT.ap()[:, :], reads=("kT",))
            dv = nc.dram_tensor("dbg_v", [128, NT * 512], BF16,
                                kind="ExternalOutput").ap()
            em.dma("gpsimd", "sdbg", dv[:, :], v_st.ap()[:, :], reads=("v_st",))

        for ti in range(NT):
            em.overlay(f"gsil{ti}", [f"wvbf{ti}"])

        for g in range(GK):
            for i in range(ND):
                em.dma("gpsimd", "swq", wq_bf.ap()[:, 512 * i:512 * (i + 1)],
                       wq_in[128 * i:128 * (i + 1), 512 * g:512 * (g + 1)],
                       writes=(f"wqbf{i}",))
            for i in range(ND):
                em.res[f"wqbf{i}"]["w"]["swq"] = em.sems["swq"].n
            pend_t = []
            for ti in range(NT):
                p = ti % 2
                qb, qkey = p, f"pb{p}"
                proj_group(qb, qkey, ti, wq_bf,
                           tuple(f"wqbf{i}" for i in range(ND)))
                if pend_t:
                    pend_t.pop(0)()
                s0 = 32 * p
                rms4(pb[qb].ap(), qkey, p, f"krs{p}")
                em.op("vector", lambda e, o=col(s0 + 21, 4),
                      i_=SCAL.ap()[:, 4 * g:4 * (g + 1)],
                      s=LOGP.ap()[:, ti:ti + 1]:
                      e.tensor_scalar(o, i_, s, None, op0=ALU.mult),
                      reads=("SCAL", "LOGP"), writes=(f"lsc{p}",))
                em.op("vector", lambda e, a_=col(s0 + 17, 4),
                      b_=col(s0 + 21, 4), o=col(s0 + 25, 4):
                      e.tensor_mul(o, a_, b_),
                      reads=(f"krs{p}", f"lsc{p}"), writes=(f"qrall{p}",))
                qn = qf.ap()[:, 512 * p:512 * (p + 1)].rearrange(
                    "p (h e) -> p h e", e=128)
                em.op("vector", lambda e, o=qn,
                      i_=pb[qb].ap()[:, :].rearrange("p (h e) -> p h e", e=128),
                      r=col(s0 + 25, 4).unsqueeze(2).broadcast_to((128, 4, 128)):
                      e.tensor_tensor(o, i_, r, op=ALU.mult),
                      reads=(qkey, f"qrall{p}"), writes=(f"qf{p}",))
                qro = sqs.ap()[:, 512 * p:512 * (p + 1)].rearrange(
                    "p (h e) -> p h e", e=128)
                rope_emit(qn, qro, ti, (f"qf{p}",), (f"sqs{p}",))
                em.op("scalar", lambda e, o=kqbf.ap()[:, 512 * p:512 * (p + 1)],
                      i_=sqs.ap()[:, 512 * p:512 * (p + 1)]: e.copy(o, i_),
                      reads=(f"sqs{p}",), writes=(f"kqbf{p}",))
                pend_t.append(lambda ti=ti: emit_kq_transpose(ti, qT, "qT"))
            while pend_t:
                pend_t.pop(0)()
            for i in range(ND):
                em.dma("gpsimd", "swq", wq_bf.ap()[:, 512 * i:512 * (i + 1)],
                       wg_in[128 * i:128 * (i + 1), 512 * g:512 * (g + 1)],
                       writes=(f"wqbf{i}",))
            for i in range(ND):
                em.res[f"wqbf{i}"]["w"]["swq"] = em.sems["swq"].n
            for ti in range(NT):
                p = ti % 2
                gb, gkey = 2 + p, f"pb{2 + p}"
                proj_group(gb, gkey, ti, wq_bf,
                           tuple(f"wqbf{i}" for i in range(ND)))
                em.op("scalar", lambda e,
                      o=gsil.ap()[:, 512 * ti:512 * (ti + 1)],
                      i_=pb[gb].ap()[:, :]: e.activation(o, i_, AF.Silu),
                      reads=(gkey,), writes=(f"gsil{ti}",))
            def unit_params(u):
                ti, h = u // 4, u % 4
                Wt = min(ti, 8) + 1
                jlo = max(0, ti - 8)
                win = 128 * Wt
                nch = (win + 511) // 512
                hp = h % 2
                return ti, h, Wt, jlo, win, nch, hp

            def emitA(u):
                ti, h, Wt, jlo, win, nch, hp = unit_params(u)
                sbanks = [3 * hp + c for c in range(nch)]
                span0 = 512 * 3 * hp
                fns = []
                for c in range(nch):
                    c0 = 512 * c
                    w = min(512, win - c0)
                    fns.append(
                        lambda e, o=pb[sbanks[c]].ap()[:, 0:w],
                        l=qT.ap()[:, 2048 * h + 128 * ti:
                                  2048 * h + 128 * ti + 128],
                        r=kT.ap()[:, 2048 * g + 128 * jlo + c0:
                                  2048 * g + 128 * jlo + c0 + w]:
                        e.matmul(o, l, r, start=True, stop=False,
                                 skip_group_check=True))
                if ti >= 8:
                    fns.append(lambda e, o=PBIG.ap()[:, span0:span0 + 128]:
                               e.matmul(o, IDENT.ap()[:, :], TRIU.ap()[:, :],
                                        start=False, stop=False,
                                        skip_group_check=True))
                fns.append(lambda e,
                           o=PBIG.ap()[:, span0 + win - 128:span0 + win]:
                           e.matmul(o, IDENT.ap()[:, :], TRIL.ap()[:, :],
                                    start=False, stop=True,
                                    skip_group_check=True))
                em.group("tensor", fns, reads=("qT", "kT", "IDENT", "TRIU",
                                               "TRIL"),
                         writes=tuple(f"pb{b_}" for b_ in sbanks))

            def emitB(u):
                ti, h, Wt, jlo, win, nch, hp = unit_params(u)
                sbanks = [3 * hp + c for c in range(nch)]
                skeys = tuple(f"pb{b_}" for b_ in sbanks)
                span = PBIG.ap()[:, 512 * 3 * hp:512 * 3 * hp + win]
                s0 = 64 + 16 * hp
                em.op("vector", lambda e, o=col(s0 + 3), i_=span:
                      e.reduce_max(o, i_, axis=AX.X, negate=True),
                      reads=skeys, writes=(f"nm{hp}",))
                em.op("scalar", lambda e,
                      o=p_bf.ap()[:, 1280 * hp:1280 * hp + win],
                      i_=span, acc=col(s0 + 8), bi=col(s0 + 3):
                      e.activation(o, i_, AF.Exp, bias=bi,
                                   accum_out=acc),
                      reads=skeys + (f"nm{hp}",),
                      writes=(f"p_bf{hp}", f"p_bf{hp}b", f"l{hp}"))
                em.op("vector", lambda e, o=col(s0 + 9 + h),
                      i_=col(s0 + 8): e.reciprocal(o, i_),
                      reads=(f"l{hp}",), writes=(f"rl_{h}",))

            def emitC(u):
                ti, h, Wt, jlo, win, nch, hp = unit_params(u)
                ykey = f"pb{6 + hp}"
                nb = (Wt + 3) // 4

                def pT_batch(bt):
                    ntile = min(4, Wt - 4 * bt)
                    tbk = 3 * hp + (bt % 3)
                    fns = [lambda e,
                           o=pb[tbk].ap()[:, 128 * k:128 * (k + 1)],
                           l=p_bf.ap()[:, 1280 * hp + 128 * (4 * bt + k):
                                       1280 * hp + 128 * (4 * bt + k + 1)]:
                           e.matmul(o, l, IDENT.ap()[:, :], start=True,
                                    stop=True, skip_group_check=True)
                           for k in range(ntile)]
                    pkeys = ((f"p_bf{hp}",) if 4 * bt + ntile <= 4
                             else (f"p_bf{hp}", f"p_bf{hp}b"))
                    em.group("tensor", fns, reads=pkeys + ("IDENT",),
                             writes=(f"pb{tbk}",))

                def evac_batch(bt):
                    ntile = min(4, Wt - 4 * bt)
                    tbk = 3 * hp + (bt % 3)
                    pt = bt % 2
                    eng = "scalar" if bt % 2 == 0 else "vector"
                    if eng == "scalar":
                        em.op("scalar", lambda e,
                              o=pT_bf.ap()[:, 512 * pt:512 * pt + 128 * ntile],
                              i_=pb[tbk].ap()[:, 0:128 * ntile]:
                              e.copy(o, i_), reads=(f"pb{tbk}",),
                              writes=(f"pT{pt}",))
                    else:
                        em.op("vector", lambda e,
                              o=pT_bf.ap()[:, 512 * pt:512 * pt + 128 * ntile],
                              i_=pb[tbk].ap()[:, 0:128 * ntile]:
                              e.tensor_copy(o, i_), reads=(f"pb{tbk}",),
                              writes=(f"pT{pt}",))

                def pv_batch(bt):
                    ntile = min(4, Wt - 4 * bt)
                    pt = bt % 2
                    fns = [lambda e, o=pb[6 + hp].ap()[:, 0:128],
                           l=pT_bf.ap()[:, 512 * pt + 128 * k:
                                        512 * pt + 128 * (k + 1)],
                           r=v_st.ap()[:, 512 * (jlo + 4 * bt + k) + 128 * g:
                                       512 * (jlo + 4 * bt + k) + 128 * (g + 1)],
                           s=(bt == 0 and k == 0),
                           st=(4 * bt + k == Wt - 1):
                           e.matmul(o, l, r, start=s, stop=st,
                                    skip_group_check=True)
                           for k in range(ntile)]
                    em.group("tensor", fns, reads=(f"pT{pt}", "v_st"),
                             writes=(ykey,))

                pT_batch(0)
                for bt in range(nb):
                    if bt + 1 < nb:
                        pT_batch(bt + 1)
                    evac_batch(bt)
                    pv_batch(bt)

            def emitD(u):
                ti, h, Wt, jlo, win, nch, hp = unit_params(u)
                ykey = f"pb{6 + hp}"
                em.op("vector", lambda e,
                      o=w_all.ap()[:, 128 * h:128 * (h + 1)],
                      i0=pb[6 + hp].ap()[:, 0:128],
                      sc=col(64 + 16 * hp + 9 + h),
                      i1=gsil.ap()[:, 512 * ti + 128 * h:
                                   512 * ti + 128 * (h + 1)]:
                      e.scalar_tensor_tensor(o, i0, sc, i1, op0=ALU.mult,
                                             op1=ALU.mult),
                      reads=(ykey, f"rl_{h}", f"gsil{ti}"),
                      writes=("w_all",))
                em.op("scalar", lambda e,
                      o=sqs.ap()[:, 128 * h:128 * (h + 1)],
                      i_=w_all.ap()[:, 128 * h:128 * (h + 1)],
                      acc=col(96 + h): e.activation(o, i_, AF.Square,
                                                    accum_out=acc),
                      reads=("w_all",), writes=("sqs0", "wss"))
                if h != 3:
                    return
                em.op("scalar", lambda e, i_=col(96, 4), o=col(100, 4):
                      e.activation(o, i_, AF.Ln, scale=1.0 / DH, bias=EPS6),
                      reads=("wss", "CEPS"), writes=("wsd",))
                em.op("scalar", lambda e, i_=col(100, 4), o=col(104, 4):
                      e.activation(o, i_, AF.Exp, scale=-0.5),
                      reads=("wsd",), writes=("rw",))
                em.op("vector", lambda e,
                      o=w_all2.ap()[:, :].rearrange("p (h e) -> p h e", e=128),
                      i_=w_all.ap()[:, :].rearrange("p (h e) -> p h e", e=128),
                      r=col(104, 4).unsqueeze(2).broadcast_to((128, 4, 128)):
                      e.tensor_tensor(o, i_, r, op=ALU.mult),
                      reads=("w_all", "rw"), writes=("w_all2",))
                sp = ti % 2
                em.op("vector", lambda e,
                      o=sstage.ap()[:, 512 * sp:512 * (sp + 1)].rearrange(
                          "p (h e) -> p h e", e=128),
                      i_=w_all2.ap()[:, :].rearrange("p (h e) -> p h e", e=128),
                      r=GN.ap()[:, :].unsqueeze(1).broadcast_to((128, 4, 128)):
                      e.tensor_tensor(o, i_, r, op=ALU.mult),
                      reads=("w_all2", "GN"), writes=(f"sstage{sp}",))
                em.dma("gpsimd", "s_sout",
                       s_scr[128 * ti:128 * (ti + 1), 512 * g:512 * (g + 1)],
                       sstage.ap()[:, 512 * sp:512 * (sp + 1)],
                       reads=(f"sstage{sp}",), writes=("s_scr",))

            NU = NT * 4
            emitA(0)
            emitB(0)
            for u in range(1, NU):
                emitA(u)
                emitC(u - 1)
                emitB(u)
                emitD(u - 1)
            emitC(NU - 1)
            emitD(NU - 1)

        if "qT" in dbg:
            dqT = nc.dram_tensor("dbg_qT", [128, 4 * T], BF16,
                                 kind="ExternalOutput").ap()
            em.dma("gpsimd", "sdbg", dqT[:, :], qT.ap()[:, :], reads=("qT",))
            dgs = nc.dram_tensor("dbg_gsil", [128, NT * 512], BF16,
                                 kind="ExternalOutput").ap()
            em.dma("gpsimd", "sdbg", dgs[:, :], gsil.ap()[:, :],
                   reads=tuple(f"gsil{t}" for t in range(NT)))

        em.barrier(extra_streams=("s_sout",))
        em.res["s_scr"] = {"w": {"s_sout": em.sems["s_sout"].n}, "r": {}}
        for key in ("IDENT", "HAD", "COS", "SIN", "LOGP", "TRIU", "TRIL",
                    "CEPS", "SCAL", "GN", "xnT"):
            em.res[key] = {"w": {}, "r": {}}

        w2all = []

        def emit_w2pre_load_and_bfly(dt):
            src = w2x if dt % 2 == 0 else w2y
            skey = "w2x" if dt % 2 == 0 else "w2y"
            scr, ckey = w2z, "w2z"
            w2all.append(lambda dt=dt, src=src, skey=skey: em.dma(
                "gpsimd", f"sw2_{dt % 2}", src.ap()[:, :],
                w2_in[128 * dt:128 * (dt + 1), :], writes=(skey,)))
            for s in range(6):
                blk = 128 * (1 << s)
                fr, to = (src, scr) if s % 2 == 0 else (scr, src)
                frk, tok = (skey, ckey) if s % 2 == 0 else (ckey, skey)
                frv = fr.ap()[:, :].rearrange("p (n two b) -> p n two b",
                                              two=2, b=blk)
                tov = to.ap()[:, :].rearrange("p (n two b) -> p n two b",
                                              two=2, b=blk)
                w2all.append(lambda frv=frv, tov=tov, frk=frk, tok=tok:
                             em.op("vector", lambda e, o=tov[:, :, 0, :],
                                   a_=frv[:, :, 0, :], b_=frv[:, :, 1, :]:
                                   e.tensor_add(o, a_, b_),
                                   reads=(frk,), writes=(tok,)))
                w2all.append(lambda frv=frv, tov=tov, frk=frk, tok=tok:
                             em.op("vector", lambda e, o=tov[:, :, 1, :],
                                   a_=frv[:, :, 0, :], b_=frv[:, :, 1, :]:
                                   e.tensor_sub(o, a_, b_),
                                   reads=(frk,), writes=(tok,)))
            for bt in range(16):
                w2all.append(lambda dt=dt, bt=bt, src=src, skey=skey:
                             emit_one_w2batch(dt, bt, src, skey))

        w2h_pend = []

        def emit_w2pre_h(item):
            dt, bt = item
            em.group("tensor",
                     [lambda e, o=pb[7].ap()[:, :], l=HAD.ap()[:, :],
                      r=w2st.ap()[:, 512 * bt:512 * (bt + 1)]:
                      e.matmul(o, l, r, start=True, stop=True,
                               skip_group_check=True)],
                     reads=(f"w2st{bt % 2}", "HAD"), writes=("pb7",))
            em.op("scalar", lambda e, o=w2st.ap()[:, 512 * bt:512 * (bt + 1)],
                  i_=pb[7].ap()[:, :]: e.activation(o, i_, AF.Copy,
                                                    scale=FWHT_SCALE),
                  reads=("pb7",), writes=(f"w2st{bt % 2}",))
            if bt == 15:
                dstv = w2pp[dt // 2].rearrange("p (a e) -> p a e", e=256)[
                    :, :, 128 * (dt % 2):128 * (dt % 2) + 128]
                srcv = w2st.ap()[:, :].rearrange("p (a e) -> p a e", e=128)
                for q4 in range(4):
                    em.dma("gpsimd", "sw2pp",
                           dstv[:, 16 * q4:16 * (q4 + 1), :],
                           srcv[:, 16 * q4:16 * (q4 + 1), :],
                           reads=("w2st0", "w2st1"), writes=("w2pp",))

        def emit_one_w2batch(dt, bt, src, skey):
            fns = [lambda e, o=pb[6].ap()[:, 128 * k:128 * (k + 1)],
                   l=src.ap()[:, 128 * (4 * bt + k):128 * (4 * bt + k + 1)]:
                   e.matmul(o, l, IDENT.ap()[:, :], start=True, stop=True,
                            skip_group_check=True) for k in range(4)]
            em.group("tensor", fns, reads=(skey, "IDENT"), writes=("pb6",))
            em.op("vector", lambda e, o=w2st.ap()[:, 512 * bt:512 * (bt + 1)],
                  i_=pb[6].ap()[:, :]: e.tensor_copy(o, i_),
                  reads=("pb6",), writes=(f"w2st{bt % 2}",))
            if w2h_pend:
                emit_w2pre_h(w2h_pend.pop(0))
            w2h_pend.append((dt, bt))
            if bt == 15:
                while w2h_pend:
                    emit_w2pre_h(w2h_pend.pop(0))

        def emit_wt(aa, nat, matT, natkey, tkey, bank, batch):
            sa_, pa = aa % 4, aa % 2
            fns = [lambda e, o=pb[bank].ap()[:, 128 * k:128 * (k + 1)],
                   l=nat.ap()[:, sa_ * D + 128 * (4 * batch + k):
                              sa_ * D + 128 * (4 * batch + k + 1)]:
                   e.matmul(o, l, IDENT.ap()[:, :], start=True, stop=True,
                            skip_group_check=True) for k in range(4)]
            em.group("tensor", fns, reads=(f"{natkey}{sa_}", "IDENT"),
                     writes=(f"pb{bank}",))
            em.op("scalar", lambda e,
                  o=matT.ap()[:, pa * D + 512 * batch:pa * D + 512 * (batch + 1)],
                  i_=pb[bank].ap()[:, :]: e.copy(o, i_),
                  reads=(f"pb{bank}",), writes=(f"{tkey}{pa}",))

        def load_w13(aa):
            sa_ = aa % 4
            em.dma("gpsimd", f"sw1_{sa_}", w1nat.ap()[:, sa_ * D:(sa_ + 1) * D],
                   w1_in[128 * aa:128 * (aa + 1), :], writes=(f"w1nat{sa_}",))
            em.dma("gpsimd", f"sw3_{sa_}", w3nat.ap()[:, sa_ * D:(sa_ + 1) * D],
                   w3_in[128 * aa:128 * (aa + 1), :], writes=(f"w3nat{sa_}",))

        load_w13(0)
        load_w13(1)
        for aa in range(NA):
            pa = aa % 2
            if aa + 2 < NA:
                load_w13(aa + 2)
            if aa % 3 == 0 and aa // 3 < ND:
                emit_w2pre_load_and_bfly(aa // 3)
            if aa == 0:
                for bt in range(4):
                    emit_wt(0, w1nat, w1T, "w1nat", "w1T", 6, bt)
                    emit_wt(0, w3nat, w3T, "w3nat", "w3T", 7, bt)
            for c in range(4):
                ub, ukey = c % 2, f"pb{c % 2}"
                fns = [lambda e, o=pb[ub].ap()[:, :],
                       l=w1T.ap()[:, pa * D + 128 * i:pa * D + 128 * (i + 1)],
                       r=xnT_c(i, c), s=(i == 0), st=(i == ND - 1):
                       e.matmul(o, l, r, start=s, stop=st,
                                skip_group_check=True) for i in range(ND)]
                em.group("tensor", fns, reads=(f"w1T{pa}", "xnT"),
                         writes=(ukey,))
                em.op("scalar", lambda e,
                      o=sg_scr.ap()[:, 512 * (c % 2):512 * (c % 2 + 1)],
                      i_=pb[ub].ap()[:, :]: e.activation(o, i_, AF.Silu),
                      reads=(ukey,), writes=(f"sg{c % 2}",))
                if aa + 1 < NA:
                    emit_wt(aa + 1, w1nat, w1T, "w1nat", "w1T", 6, c)
                lb, lkey = 2 + c % 2, f"pb{2 + c % 2}"
                fns = [lambda e, o=pb[lb].ap()[:, :],
                       l=w3T.ap()[:, pa * D + 128 * i:pa * D + 128 * (i + 1)],
                       r=xnT_c(i, c), s=(i == 0), st=(i == ND - 1):
                       e.matmul(o, l, r, start=s, stop=st,
                                skip_group_check=True) for i in range(ND)]
                em.group("tensor", fns, reads=(f"w3T{pa}", "xnT"),
                         writes=(lkey,))
                em.op("vector", lambda e,
                      o=hidb.ap()[:, pa * D + 512 * c:pa * D + 512 * (c + 1)],
                      a_=sg_scr.ap()[:, 512 * (c % 2):512 * (c % 2 + 1)],
                      b_=pb[lb].ap()[:, :]: e.tensor_mul(o, a_, b_),
                      reads=(f"sg{c % 2}", lkey), writes=(f"hidb{pa}",))
                if aa + 1 < NA:
                    emit_wt(aa + 1, w3nat, w3T, "w3nat", "w3T", 7, c)
                em.dma("gpsimd", "shid_out", hid_scr[c, aa][:, :],
                       hidb.ap()[:, pa * D + 512 * c:pa * D + 512 * (c + 1)],
                       reads=(f"hidb{pa}",), writes=("hid_scr",))
                for _ in range(2):
                    if w2all:
                        w2all.pop(0)()
        while w2all:
            w2all.pop(0)()
        while w2h_pend:
            emit_w2pre_h(w2h_pend.pop(0))

        em.barrier(extra_streams=("shid_out", "sw2pp", "s_sout"))
        for nm, src, shape in (("s", s_scr, [NT * 128, D]),
                               ("hid", hid_scr, [4, NA, 128, 512])):
            if nm in dbg:
                dd = nc.dram_tensor(f"dbg_{nm}", shape, BF16,
                                    kind="ExternalOutput").ap()
                key = {"s": "s_scr", "hid": "hid_scr", "w2pp": "w2pp"}[nm]
                em.dma("gpsimd", "sdbg", dd[:, :] if len(shape) == 2
                       else dd[:, :, :, :], src[:, :] if len(shape) == 2
                       else src[:, :, :, :], reads=(key,))
        em.res["hid_scr"] = {"w": {"shid_out": em.sems["shid_out"].n}, "r": {}}
        em.res["w2pp"] = {"w": {"sw2pp": em.sems["sw2pp"].n}, "r": {}}
        em.res["s_scr"] = {"w": {"s_sout": em.sems["s_sout"].n}, "r": {}}

        for c in range(4):
            for dc0 in range(2):
                w2cb0 = w2c0 if dc0 == 0 else w2c1
                for sl in range(4):
                    seg = HF // 2
                    em.dma(("sync", "scalar")[sl % 2],
                           (f"sw2c{dc0}", f"sw2cs{dc0}")[sl % 2],
                           w2cb0.ap()[:, seg * sl:seg * (sl + 1)],
                           w2pp[dc0][:, seg * sl:seg * (sl + 1)],
                           reads=("w2pp",), writes=(f"w2cb{dc0}",))
                em.res[f"w2cb{dc0}"]["w"][f"sw2c{dc0}"] = \
                    em.sems[f"sw2c{dc0}"].n
                em.res[f"w2cb{dc0}"]["w"][f"sw2cs{dc0}"] = \
                    em.sems[f"sw2cs{dc0}"].n
            for k in range(8):
                em.dma("sync", f"shid_in{k // 2}",
                       hidc.ap()[:, :].rearrange("p (a e) -> p a e", e=512)[
                           :, 8 * k:8 * (k + 1), :],
                       hid_scr[c, 8 * k:8 * (k + 1)].transpose([1, 0, 2]),
                       reads=("hid_scr",), writes=(f"hidcQ{k // 2}",))
            for t1 in range(4):
                ti = 4 * c + t1
                em.dma("sync", f"sxr{t1}",
                       outstage.ap()[:, D * t1:D * (t1 + 1)],
                       x_in[128 * ti:128 * (ti + 1), :], writes=(f"os{t1}",))
            for dt in range(ND):
                em.dma("sync", "ss_in",
                       sTb.ap()[:, 512 * dt:512 * (dt + 1)],
                       s_scr[512 * c:512 * (c + 1),
                             128 * dt:128 * (dt + 1)],
                       reads=("s_scr",), writes=("sTb",), transpose=True)
            for dc in range(8):
                pd = dc % 2
                w2cb = w2c0 if pd == 0 else w2c1
                w2key = f"w2cb{pd}"
                if dc >= 2:
                    for sl in range(4):
                        seg = HF // 2
                        em.dma(("sync", "scalar")[sl % 2],
                               (f"sw2c{pd}", f"sw2cs{pd}")[sl % 2],
                               w2cb.ap()[:, seg * sl:seg * (sl + 1)],
                               w2pp[dc][:, seg * sl:seg * (sl + 1)],
                               reads=("w2pp",), writes=(w2key,))
                    em.res[w2key]["w"][f"sw2c{pd}"] = em.sems[f"sw2c{pd}"].n
                    em.res[w2key]["w"][f"sw2cs{pd}"] = \
                        em.sems[f"sw2cs{pd}"].n
                for db in range(2):
                    dt_g = 2 * dc + db
                    bank = 2 * pd + db
                    bkey = f"pb{bank}"
                    for q in range(4):
                        fns = [lambda e, o=pb[bank].ap()[:, :],
                               l=w2cb.ap()[:, 256 * a2 + 128 * db:
                                           256 * a2 + 128 * (db + 1)],
                               r=hidc.ap()[:, 512 * a2:512 * (a2 + 1)],
                               s=(a2 == 0), st=(a2 == NA - 1):
                               e.matmul(o, l, r, start=s, stop=st,
                                        skip_group_check=True)
                               for a2 in range(16 * q, 16 * (q + 1))]
                        em.group("tensor", fns,
                                 reads=(f"hidcQ{q}", w2key), writes=(bkey,))
                    gp = dt_g % 2
                    em.op("scalar", lambda e,
                          o=gfb.ap()[:, 512 * gp:512 * (gp + 1)],
                          i_=pb[bank].ap()[:, :]:
                          e.activation(o, i_, AF.Silu),
                          reads=(bkey,), writes=(f"gf{gp}",))
                    em.op("vector", lambda e,
                          o=wbf.ap()[:, 512 * gp:512 * (gp + 1)],
                          a_=gfb.ap()[:, 512 * gp:512 * (gp + 1)],
                          b_=sTb.ap()[:, 512 * dt_g:512 * (dt_g + 1)]:
                          e.tensor_mul(o, a_, b_),
                          reads=(f"gf{gp}", "sTb"), writes=(f"wbf{gp}",))
                    tbank = 4 + gp
                    fns = [lambda e,
                           o=pb[tbank].ap()[:, 128 * k:128 * (k + 1)],
                           l=wbf.ap()[:, 512 * gp + 128 * k:
                                      512 * gp + 128 * (k + 1)]:
                           e.matmul(o, l, IDENT.ap()[:, :], start=True,
                                    stop=True, skip_group_check=True)
                           for k in range(4)]
                    em.group("tensor", fns, reads=(f"wbf{gp}", "IDENT"),
                             writes=(f"pb{tbank}",))
                    for k in range(4):
                        em.op("vector", lambda e,
                              o=outstage.ap()[:, D * k + 128 * dt_g:
                                              D * k + 128 * (dt_g + 1)],
                              i_=pb[tbank].ap()[:, 128 * k:128 * (k + 1)]:
                              e.tensor_add(o, o, i_),
                              reads=(f"pb{tbank}", f"os{k}"),
                              writes=(f"os{k}",))
            for t1 in range(4):
                ti = 4 * c + t1
                em.dma("scalar", f"sout{t1}",
                       out_d[128 * ti:128 * (ti + 1), :],
                       outstage.ap()[:, D * t1:D * (t1 + 1)],
                       reads=(f"os{t1}",))

        for t1 in range(4):
            em.wait("sync", f"sout{t1}", em.sems[f"sout{t1}"].n)

        with nc.Block() as block:
            @block.sync
            def _(e):
                for f in em.ops["sync"]:
                    f(e)

            @block.gpsimd
            def _(e):
                for f in em.ops["gpsimd"]:
                    f(e)

            @block.vector
            def _(e):
                for f in em.ops["vector"]:
                    f(e)

            @block.scalar
            def _(e):
                for f in em.ops["scalar"]:
                    f(e)

            @block.tensor
            def _(e):
                for f in em.ops["tensor"]:
                    f(e)

    return nc


def kernel(L0_attn_hidden_states, L0_attn_Wq, L0_attn_Wk, L0_attn_Wv,
           L0_attn_Wg, L0_attn_softmax_scaler, L0_attn_gn_weight,
           L0_ffn_W1, L0_ffn_W2, L0_ffn_W3):
    x = np.ascontiguousarray(np.asarray(L0_attn_hidden_states, np.float32))
    wq = np.ascontiguousarray(np.asarray(L0_attn_Wq, np.float32))
    wk = np.ascontiguousarray(np.asarray(L0_attn_Wk, np.float32))
    wv = np.ascontiguousarray(np.asarray(L0_attn_Wv, np.float32))
    wg = np.ascontiguousarray(np.asarray(L0_attn_Wg, np.float32))
    w1 = np.ascontiguousarray(np.asarray(L0_ffn_W1, np.float32))
    w2 = np.ascontiguousarray(np.asarray(L0_ffn_W2, np.float32))
    w3 = np.ascontiguousarray(np.asarray(L0_ffn_W3, np.float32))
    scal = np.ascontiguousarray(np.broadcast_to(
        np.asarray(L0_attn_softmax_scaler, np.float32)[None, :], (128, H)))
    gn = np.ascontiguousarray(np.broadcast_to(
        np.asarray(L0_attn_gn_weight, np.float32)[None, :], (128, DH)))

    nc = build_program()
    in_maps = [{
        "x": np.ascontiguousarray(x[b]),
        "wq": wq, "wk": wk, "wv": wv, "wg": wg,
        "scaler_rep": scal, "gn_rep": gn,
        "w1": w1, "w2": w2, "w3": w3,
    } for b in range(8)]
    res = run_bass_kernel_spmd(nc, in_maps, core_ids=list(range(8)))
    out = np.stack([res.results[b]["out"] for b in range(8)], axis=0)
    return out.astype(np.float32)


if __name__ == "__main__":
    nc = build_program()
    print("build ok; instructions:", nc.n_instructions())


# version 34
# speedup vs baseline: 1.0007x; 1.0007x over previous
import math
import contextlib
import numpy as np
import ml_dtypes
import concourse.bass as bass
import concourse.mybir as mybir
import concourse.bass_utils as _bu
from concourse.bass_utils import run_bass_kernel_spmd

if not hasattr(_bu, "_orig_run_command"):
    _bu._orig_run_command = _bu.run_command


def _run_command_ldwopt(argv, **kwargs):
    argv = ["--enable-ldw-opt=true" if a == "--enable-ldw-opt=false" else a
            for a in argv]
    return _bu._orig_run_command(argv, **kwargs)


_bu.run_command = _run_command_ldwopt

F32 = mybir.dt.float32
BF16 = mybir.dt.bfloat16
AF = mybir.ActivationFunctionType
ALU = mybir.AluOpType
AX = mybir.AxisListType

T = 2048
D = 2048
NT = 16
ND = 16
H = 16
GK = 4
DH = 128
HF = 8192
NA = 64
INV_SQRT_DH = 1.0 / math.sqrt(DH)
FWHT_SCALE = 1.0 / math.sqrt(HF)
NEG = -1e30
BF = ml_dtypes.bfloat16


def _consts():
    i = np.arange(128)
    pop = np.zeros((128, 128), np.int64)
    for r in range(128):
        pop[r] = [bin(r & j).count("1") & 1 for j in i]
    had = (1.0 - 2.0 * pop).astype(np.float32)
    ident = np.eye(128, dtype=np.float32)
    inv_freq = 10000.0 ** (-np.arange(0, 128, 2, np.float64) / 128.0)
    fr = np.outer(np.arange(T, dtype=np.float64), inv_freq)
    cosl = np.cos(fr).astype(np.float32).reshape(NT, 128, 64).transpose(
        1, 0, 2).reshape(128, NT * 64).copy()
    sinl = np.sin(fr).astype(np.float32).reshape(NT, 128, 64).transpose(
        1, 0, 2).reshape(128, NT * 64).copy()
    logp = (np.log(np.arange(1, T + 1, dtype=np.float64)) / np.sqrt(128.0)
            ).astype(np.float32).reshape(NT, 128).T.copy()
    ii = np.arange(128)[:, None]
    jj = np.arange(128)[None, :]
    triu = np.where(jj >= ii, 0.0, NEG).astype(np.float32)
    tril = np.where(jj <= ii, 0.0, NEG).astype(np.float32)
    ceps = np.zeros((128, 8), np.float32)
    ceps[:, 0] = 1e-6
    ceps[:, 1] = 1e-5
    ceps[:, 2] = 1.0
    return dict(had=had.astype(BF), ident=ident.astype(BF), cos=cosl, sin=sinl,
                logp=logp, triu=triu.astype(BF), tril=tril.astype(BF),
                ceps=ceps)


class Sem:
    def __init__(self, ctx, nc, name):
        self.h = ctx.enter_context(nc.semaphore(name))
        self.n = 0
        self.name = name


class Em:

    ENGINES = ("tensor", "vector", "scalar", "sync", "gpsimd")

    def __init__(self, ctx, nc):
        self.nc = nc
        self.ctx = ctx
        self.ops = {e: [] for e in self.ENGINES}
        self.prog = {e: Sem(ctx, nc, f"pg_{e}") for e in self.ENGINES}
        self.res = {}
        self.cover = {e: {} for e in self.ENGINES}
        self.sems = {}
        self.nwaits = 0

    def stream_sem(self, name):
        if name not in self.sems:
            self.sems[name] = Sem(self.ctx, self.nc, name)
        return self.sems[name]

    def _handle(self, name):
        if name in self.sems:
            return self.sems[name].h
        for e in self.ENGINES:
            if self.prog[e].name == name:
                return self.prog[e].h
        raise KeyError(name)

    def _needs(self, eng, reads, writes):
        need = {}

        def add(sn, v):
            if self.cover[eng].get(sn, -1) >= v:
                return
            if need.get(sn, -1) < v:
                need[sn] = v

        for k in reads:
            st = self.res.get(k)
            if st:
                for sn, v in st["w"].items():
                    add(sn, v)
        for k in writes:
            st = self.res.get(k)
            if st:
                for sn, v in st["w"].items():
                    add(sn, v)
                for sn, v in st["r"].items():
                    add(sn, v)
        return need

    def _commit(self, eng, need, semname, semval, reads, writes):
        for sn, v in need.items():
            if self.cover[eng].get(sn, -1) < v:
                self.cover[eng][sn] = v
        for k in writes:
            self.res[k] = {"w": {semname: semval}, "r": {}}
        for k in reads:
            st = self.res.setdefault(k, {"w": {}, "r": {}})
            if st["r"].get(semname, -1) < semval:
                st["r"][semname] = semval

    def _mk_emit(self, fns, need):
        handles = {n: self._handle(n) for n in need}
        self.nwaits += len(need)

        def emit(e, fns=fns, need=dict(need), handles=handles):
            for n, v in need.items():
                e.wait_ge(handles[n], v)
            last = None
            for f in fns:
                last = f(e)
            return last

        return emit

    def op(self, eng, fn, reads=(), writes=()):
        self.group(eng, [fn], reads, writes)

    def group(self, eng, fns, reads=(), writes=()):
        need = self._needs(eng, reads, writes)
        sem = self.prog[eng]
        sem.n += 1
        inner = self._mk_emit(fns, need)

        def emit(e, inner=inner, sem=sem):
            inner(e).then_inc(sem.h, 1)

        self.ops[eng].append(emit)
        self._commit(eng, need, sem.name, sem.n, reads, writes)

    def dma(self, eng, stream, out, in_, reads=(), writes=(), transpose=False):
        ssem = self.stream_sem(stream)
        need = self._needs(eng, reads, writes)
        ssem.n += 16
        inner = self._mk_emit(
            [lambda e, out=out, in_=in_, tr=transpose:
             e.dma_start(out, in_, transpose=tr)], need)

        def emit(e, inner=inner, ssem=ssem):
            inner(e).then_inc(ssem.h, 16)

        self.ops[eng].append(emit)
        for sn, v in need.items():
            if self.cover[eng].get(sn, -1) < v:
                self.cover[eng][sn] = v
        for k in writes:
            self.res[k] = {"w": {stream: ssem.n}, "r": {}}
        for k in reads:
            st = self.res.setdefault(k, {"w": {}, "r": {}})
            if st["r"].get(stream, -1) < ssem.n:
                st["r"][stream] = ssem.n

    def wait(self, eng, semname, val):
        h = self._handle(semname)
        self.ops[eng].append(lambda e, h=h, val=val: e.wait_ge(h, val))
        if self.cover[eng].get(semname, -1) < val:
            self.cover[eng][semname] = val

    def overlay(self, newkey, oldkeys):
        st = {"w": {}, "r": {}}
        for k in oldkeys:
            o = self.res.pop(k, None)
            if not o:
                continue
            for sn, v in o["w"].items():
                st["r"][sn] = max(st["r"].get(sn, -1), v)
            for sn, v in o["r"].items():
                st["r"][sn] = max(st["r"].get(sn, -1), v)
        self.res[newkey] = st

    def barrier(self, extra_streams=()):
        targets = [(self.prog[e].name, self.prog[e].n)
                   for e in ("tensor", "vector", "scalar")]
        targets += [(s, self.sems[s].n) for s in extra_streams if s in self.sems]
        for e in self.ENGINES:
            for sn, v in targets:
                if v > 0 and self.cover[e].get(sn, -1) < v:
                    self.wait(e, sn, v)
        self.res = {}


def build_program(dbg=()):
    C = _consts()
    nc = bass.Bass("TRN2", target_bir_lowering=False)

    x_in = nc.dram_tensor("x", [T, D], F32, kind="ExternalInput").ap()
    wq_in = nc.dram_tensor("wq", [D, H * DH], F32, kind="ExternalInput").ap()
    wk_in = nc.dram_tensor("wk", [D, GK * DH], F32, kind="ExternalInput").ap()
    wv_in = nc.dram_tensor("wv", [D, GK * DH], F32, kind="ExternalInput").ap()
    wg_in = nc.dram_tensor("wg", [D, H * DH], F32, kind="ExternalInput").ap()
    scal_in = nc.dram_tensor("scaler_rep", [128, H], F32, kind="ExternalInput").ap()
    gn_in = nc.dram_tensor("gn_rep", [128, DH], F32, kind="ExternalInput").ap()
    w1_in = nc.dram_tensor("w1", [HF, D], F32, kind="ExternalInput").ap()
    w2_in = nc.dram_tensor("w2", [D, HF], F32, kind="ExternalInput").ap()
    w3_in = nc.dram_tensor("w3", [HF, D], F32, kind="ExternalInput").ap()
    out_d = nc.dram_tensor("out", [T, D], F32, kind="ExternalOutput").ap()

    s_scr = nc.dram_tensor("s_scr", [NT * 128, D], BF16, kind="Internal").ap()
    hid_scr = nc.dram_tensor("hid_scr", [4, NA, 128, 512], BF16,
                             kind="Internal").ap()
    w2pp = nc.dram_tensor("w2pp", [8, 128, 2 * HF], BF16, kind="Internal").ap()

    cin = {}
    for nm, arr in (("ident", C["ident"]), ("had", C["had"]), ("cos", C["cos"]),
                    ("sin", C["sin"]), ("logp", C["logp"]), ("triu", C["triu"]),
                    ("tril", C["tril"]), ("ceps", C["ceps"])):
        cin[nm] = nc.inline_tensor(arr, name=f"c_{nm}").ap()

    dbg_outs = {}

    ctx = contextlib.ExitStack()
    with ctx:
        em = Em(ctx, nc)

        def sball(name, cols, dtype, offset):
            return nc.alloc_sbuf_tensor_at(name, [128, cols], dtype,
                                           offset=offset)

        off = int(nc._sbuf_addr_for_side("left"))

        def nxt(cols, dtype, align=64):
            nonlocal off
            esz = 2 if dtype == BF16 else 4
            off = (off + align - 1) // align * align
            o = off
            off += cols * esz
            return o

        IDENT = sball("IDENT", 128, BF16, nxt(128, BF16))
        HAD = sball("HAD", 128, BF16, nxt(128, BF16))
        COS = sball("COS", NT * 64, F32, nxt(NT * 64, F32))
        SIN = sball("SIN", NT * 64, F32, nxt(NT * 64, F32))
        LOGP = sball("LOGP", NT, F32, nxt(NT, F32))
        TRIU = sball("TRIU", 128, BF16, nxt(128, BF16))
        TRIL = sball("TRIL", 128, BF16, nxt(128, BF16))
        CEPS = sball("CEPS", 8, F32, nxt(8, F32))
        SCAL = sball("SCAL", H, F32, nxt(H, F32))
        GN = sball("GN", DH, F32, nxt(DH, F32))
        ST = sball("ST", 128, F32, nxt(128, F32))
        xnT_off = nxt(ND * T, BF16)
        xnT = sball("xnT", ND * T, BF16, xnT_off)
        arena0 = off

        offA = arena0

        def nxtA(cols, dtype, align=64):
            nonlocal offA
            esz = 2 if dtype == BF16 else 4
            offA = (offA + align - 1) // align * align
            o = offA
            offA += cols * esz
            return o

        kT = sball("kT", GK * T, BF16, nxtA(GK * T, BF16))
        v_st = sball("v_st", NT * 512, BF16, nxtA(NT * 512, BF16))
        wq_bf = sball("wq_bf", ND * 512, BF16, nxtA(ND * 512, BF16))
        qT = sball("qT", 4 * T, BF16, nxtA(4 * T, BF16))
        gsil = sball("gsil", NT * 512, BF16, nxtA(NT * 512, BF16))
        p_bf = sball("p_bf", 2 * 1280, BF16, nxtA(2 * 1280, BF16))
        pT_bf = sball("pT_bf", 2 * 512, BF16, nxtA(2 * 512, BF16))
        qf = sball("qf", 2 * 512, F32, nxtA(2 * 512, F32))
        sqs = sball("sqs", 2 * 512, F32, nxtA(2 * 512, F32))
        rtmp = sball("rtmp", 512, F32, nxtA(512, F32))
        w_all = sball("w_all", 512, F32, nxtA(512, F32))
        w_all2 = sball("w_all2", 512, F32, nxtA(512, F32))
        sstage = sball("sstage", 2 * 512, BF16, nxtA(2 * 512, BF16))
        kqbf = sball("kqbf", 2 * 512, BF16, nxtA(2 * 512, BF16))
        endA = offA

        xt = sball("xt", 2 * D, F32, arena0)
        ln_scr = sball("ln_scr", D, F32, arena0 + 16384)
        xnbf = sball("xnbf", 2 * D, BF16, arena0 + 24576)

        offB = arena0

        def nxtB(cols, dtype, align=64):
            nonlocal offB
            esz = 2 if dtype == BF16 else 4
            offB = (offB + align - 1) // align * align
            o = offB
            offB += cols * esz
            return o

        w1nat = sball("w1nat", 4 * D, BF16, nxtB(4 * D, BF16))
        w3nat = sball("w3nat", 4 * D, BF16, nxtB(4 * D, BF16))
        w1T = sball("w1T", 2 * D, BF16, nxtB(2 * D, BF16))
        w3T = sball("w3T", 2 * D, BF16, nxtB(2 * D, BF16))
        hidb = sball("hidb", 2 * D, BF16, nxtB(2 * D, BF16))
        sg_scr = sball("sg_scr", 2 * 512, F32, nxtB(2 * 512, F32))
        w2x = sball("w2x", HF, BF16, nxtB(HF, BF16))
        w2y = sball("w2y", HF, BF16, nxtB(HF, BF16))
        w2z = sball("w2z", HF, BF16, nxtB(HF, BF16))
        w2st = sball("w2st", HF, BF16, nxtB(HF, BF16))
        endB = offB

        offC = xnT_off

        def nxtC(cols, dtype, align=64):
            nonlocal offC
            esz = 2 if dtype == BF16 else 4
            offC = (offC + align - 1) // align * align
            o = offC
            offC += cols * esz
            return o

        hidc = sball("hidc", NA * 512, BF16, nxtC(NA * 512, BF16))
        w2c0 = sball("w2c0", 2 * HF, BF16, nxtC(2 * HF, BF16))
        w2c1 = sball("w2c1", 2 * HF, BF16, nxtC(2 * HF, BF16))
        sTb = sball("sTb", ND * 512, BF16, nxtC(ND * 512, BF16))
        outstage = sball("outstage", 4 * D, F32, nxtC(4 * D, F32))
        gfb = sball("gfb", 2 * 512, F32, nxtC(2 * 512, F32))
        wbf = sball("wbf", 2 * 512, BF16, nxtC(2 * 512, BF16))
        endC = offC

        PBIG = ctx.enter_context(nc.psum_tensor("PBIG", [128, 4096], F32))

        class BankView:
            def __init__(self, k):
                self.k = k

            def ap(self):
                return PBIG.ap()[:, 512 * self.k:512 * (self.k + 1)]

        pb = [BankView(i) for i in range(8)]

        for apv, sbuf, key in (
            (cin["ident"], IDENT, "IDENT"), (cin["had"], HAD, "HAD"),
            (cin["cos"], COS, "COS"), (cin["sin"], SIN, "SIN"),
            (cin["logp"], LOGP, "LOGP"), (cin["triu"], TRIU, "TRIU"),
            (cin["tril"], TRIL, "TRIL"), (cin["ceps"], CEPS, "CEPS"),
            (scal_in, SCAL, "SCAL"), (gn_in, GN, "GN"),
        ):
            em.dma("sync", "scst", sbuf.ap()[:, :], apv[:, :], writes=(key,))

        for key in ("IDENT", "HAD", "COS", "SIN", "LOGP", "TRIU", "TRIL",
                    "CEPS", "SCAL", "GN"):
            em.res[key]["w"]["scst"] = em.sems["scst"].n

        EPS6 = CEPS.ap()[:, 0:1]
        EPS5 = CEPS.ap()[:, 1:2]

        def col(c0, w=1):
            return ST.ap()[:, c0:c0 + w]

        def xnT_t(i, ti):
            return xnT.ap()[:, 2048 * i + 128 * ti:2048 * i + 128 * (ti + 1)]

        def xnT_c(i, c):
            return xnT.ap()[:, 2048 * i + 512 * c:2048 * i + 512 * (c + 1)]

        p0_pend = []

        def p0_transpose(ti, p):
            for bt in range(4):
                bk = 4 + bt % 2
                fns = [lambda e, o=pb[bk].ap()[:, 128 * k:128 * (k + 1)],
                       l=xnbf.ap()[:, p * D + 128 * (4 * bt + k):
                                   p * D + 128 * (4 * bt + k + 1)]:
                       e.matmul(o, l, IDENT.ap()[:, :], start=True, stop=True,
                                skip_group_check=True) for k in range(4)]
                em.group("tensor", fns, reads=(f"xnbf{p}", "IDENT"),
                         writes=(f"pb{bk}",))
                dst = xnT.ap()[:, :].rearrange("p (i t) -> p i t", t=T)[
                    :, 4 * bt:4 * (bt + 1), 128 * ti:128 * (ti + 1)]
                em.op("vector", lambda e, d=dst,
                      s=pb[bk].ap()[:, :].rearrange("p (i t) -> p i t",
                                                    t=128):
                      e.tensor_copy(d, s), reads=(f"pb{bk}",),
                      writes=("xnT",))

        for ti in range(NT):
            p = ti % 2
            xtile = xt.ap()[:, p * D:(p + 1) * D]
            kx = f"xt{p}"
            em.dma("sync", f"sx{p}", xtile, x_in[128 * ti:128 * (ti + 1), :],
                   writes=(kx,))
            s0 = 32 * p
            em.op("vector", lambda e, i_=xtile, o=col(s0 + 0):
                  e.reduce_sum(o, i_, axis=AX.X), reads=(kx,),
                  writes=(f"s1_{p}",))
            em.op("scalar", lambda e, i_=xtile, o=ln_scr.ap()[:, :],
                  acc=col(s0 + 1): e.activation(o, i_, AF.Square, accum_out=acc),
                  reads=(kx,), writes=("ln_scr", f"s2_{p}"))
            em.op("scalar", lambda e, i_=col(s0 + 0), o=col(s0 + 2):
                  e.mul(o, i_, 1.0 / D), reads=(f"s1_{p}",), writes=(f"m_{p}",))
            em.op("vector", lambda e, m=col(s0 + 2), o=col(s0 + 3):
                  e.tensor_mul(o, m, m), reads=(f"m_{p}",), writes=(f"msq_{p}",))
            em.op("vector", lambda e, s2=col(s0 + 1), msq=col(s0 + 3),
                  o=col(s0 + 4): e.tensor_scalar(o, s2, 1.0 / D, msq,
                                                 op0=ALU.mult, op1=ALU.subtract),
                  reads=(f"s2_{p}", f"msq_{p}"), writes=(f"var_{p}",))
            em.op("scalar", lambda e, v=col(s0 + 4), o=col(s0 + 5):
                  e.activation(o, v, AF.Ln, bias=EPS5),
                  reads=(f"var_{p}", "CEPS"), writes=(f"sd_{p}",))
            em.op("scalar", lambda e, i_=col(s0 + 5), o=col(s0 + 6):
                  e.activation(o, i_, AF.Exp, scale=-0.5),
                  reads=(f"sd_{p}",), writes=(f"rstd_{p}",))
            em.op("vector", lambda e, m=col(s0 + 2), r=col(s0 + 6),
                  o=col(s0 + 7): e.tensor_mul(o, m, r),
                  reads=(f"m_{p}", f"rstd_{p}"), writes=(f"mrs_{p}",))
            em.op("scalar", lambda e, i_=col(s0 + 7), o=col(s0 + 8):
                  e.mul(o, i_, -1.0), reads=(f"mrs_{p}",), writes=(f"nmrs_{p}",))
            xnb = xnbf.ap()[:, p * D:(p + 1) * D]
            em.op("vector", lambda e, i_=xtile, o=xnb, sc=col(s0 + 6),
                  bi=col(s0 + 8): e.tensor_scalar(o, i_, sc, bi, op0=ALU.mult,
                                                  op1=ALU.add),
                  reads=(kx, f"rstd_{p}", f"nmrs_{p}"), writes=(f"xnbf{p}",))
            if p0_pend:
                p0_pend.pop(0)()
            p0_pend.append(lambda ti=ti, p=p: p0_transpose(ti, p))
            continue
        _unused = None
        for _never in ():
            for bt in range(4):
                bk = 4 + bt % 2
                fns = [lambda e, o=pb[bk].ap()[:, 128 * k:128 * (k + 1)],
                       l=xnbf.ap()[:, p * D + 128 * (4 * bt + k):
                                   p * D + 128 * (4 * bt + k + 1)]:
                       e.matmul(o, l, IDENT.ap()[:, :], start=True, stop=True,
                                skip_group_check=True) for k in range(4)]
                em.group("tensor", fns, reads=(f"xnbf{p}", "IDENT"),
                         writes=(f"pb{bk}",))
                dst = xnT.ap()[:, :].rearrange("p (i t) -> p i t", t=T)[
                    :, 4 * bt:4 * (bt + 1), 128 * ti:128 * (ti + 1)]
                em.op("vector", lambda e, d=dst,
                      s=pb[bk].ap()[:, :].rearrange("p (i t) -> p i t", t=128):
                      e.tensor_copy(d, s), reads=(f"pb{bk}",), writes=("xnT",))

        while p0_pend:
            p0_pend.pop(0)()

        if "xnT" in dbg:
            dxnT = nc.dram_tensor("dbg_xnT", [128, ND * T], BF16,
                                  kind="ExternalOutput").ap()
            em.dma("gpsimd", "sdbg", dxnT[:, :], xnT.ap()[:, :],
                   reads=("xnT",))

        for i in range(ND):
            em.dma("gpsimd", "skv", wq_bf.ap()[:, 512 * i:512 * (i + 1)],
                   wk_in[128 * i:128 * (i + 1), :], writes=(f"wqbf{i}",))
        for i in range(ND):
            em.dma("gpsimd", "skv", gsil.ap()[:, 512 * i:512 * (i + 1)],
                   wv_in[128 * i:128 * (i + 1), :], writes=(f"wvbf{i}",))
        for i in range(ND):
            em.res[f"wqbf{i}"]["w"]["skv"] = em.sems["skv"].n
            em.res[f"wvbf{i}"]["w"]["skv"] = em.sems["skv"].n

        em.barrier(extra_streams=())
        em.res["xnT"] = {"w": {"pg_vector": em.prog["vector"].n}, "r": {}}
        for i in range(ND):
            em.res[f"wqbf{i}"] = {"w": {"skv": 16 * 2 * ND}, "r": {}}
            em.res[f"wvbf{i}"] = {"w": {"skv": 16 * 2 * ND}, "r": {}}
        for key in ("IDENT", "HAD", "COS", "SIN", "LOGP", "TRIU", "TRIL",
                    "CEPS", "SCAL", "GN"):
            em.res[key] = {"w": {"scst": em.sems["scst"].n}, "r": {}}

        def rope_emit(srcv, dstv, ti, rkeys, wkeys):
            cosv = COS.ap()[:, 64 * ti:64 * (ti + 1)].unsqueeze(1).broadcast_to(
                (128, 4, 64))
            sinv = SIN.ap()[:, 64 * ti:64 * (ti + 1)].unsqueeze(1).broadcast_to(
                (128, 4, 64))
            x1 = srcv[:, :, 0:64]
            x2 = srcv[:, :, 64:128]
            y1 = dstv[:, :, 0:64]
            y2 = dstv[:, :, 64:128]
            t1 = rtmp.ap()[:, 0:256].rearrange("p (h e) -> p h e", e=64)
            t2 = rtmp.ap()[:, 256:512].rearrange("p (h e) -> p h e", e=64)
            em.op("vector", lambda e: e.tensor_mul(t1, x1, cosv),
                  reads=rkeys + ("COS",), writes=("rt1",))
            em.op("vector", lambda e: e.tensor_mul(t2, x2, sinv),
                  reads=rkeys + ("SIN",), writes=("rt2",))
            em.op("vector", lambda e: e.tensor_add(y1, t1, t2),
                  reads=("rt1", "rt2"), writes=wkeys)
            em.op("vector", lambda e: e.tensor_mul(t1, x2, cosv),
                  reads=rkeys + ("COS",), writes=("rt1",))
            em.op("vector", lambda e: e.tensor_mul(t2, x1, sinv),
                  reads=rkeys + ("SIN",), writes=("rt2",))
            em.op("vector", lambda e: e.tensor_sub(y2, t1, t2),
                  reads=("rt1", "rt2"), writes=wkeys)

        def proj_group(bank, bkey, ti, rhs_handle, rkeys):
            fns = [lambda e, o=pb[bank].ap()[:, :], l=xnT_t(i, ti),
                   r=rhs_handle.ap()[:, 512 * i:512 * (i + 1)],
                   s=(i == 0), st=(i == ND - 1):
                   e.matmul(o, l, r, start=s, stop=st, skip_group_check=True)
                   for i in range(ND)]
            em.group("tensor", fns, reads=rkeys + ("xnT",), writes=(bkey,))

        def rms4(bank_ap, bkey, p, out_keybase):
            s0 = 32 * p
            for h in range(4):
                em.op("scalar", lambda e,
                      o=sqs.ap()[:, 512 * p + 128 * h:512 * p + 128 * (h + 1)],
                      i_=bank_ap[:, 128 * h:128 * (h + 1)],
                      acc=col(s0 + 9 + h):
                      e.activation(o, i_, AF.Square, accum_out=acc),
                      reads=(bkey,), writes=(f"sqs{p}", f"kss{p}"))
            em.op("scalar", lambda e, i_=col(s0 + 9, 4), o=col(s0 + 13, 4):
                  e.activation(o, i_, AF.Ln, scale=1.0 / DH, bias=EPS6),
                  reads=(f"kss{p}", "CEPS"), writes=(f"ksd{p}",))
            em.op("scalar", lambda e, i_=col(s0 + 13, 4), o=col(s0 + 17, 4):
                  e.activation(o, i_, AF.Exp, scale=-0.5),
                  reads=(f"ksd{p}",), writes=(out_keybase,))

        def emit_kq_transpose(ti, dst_handle, dst_key):
            p = ti % 2
            tb = 4 + p
            fns = [lambda e, o=pb[tb].ap()[:, 128 * h:128 * (h + 1)],
                   l=kqbf.ap()[:, 512 * p + 128 * h:512 * p + 128 * (h + 1)]:
                   e.matmul(o, l, IDENT.ap()[:, :], start=True, stop=True,
                            skip_group_check=True) for h in range(4)]
            em.group("tensor", fns, reads=(f"kqbf{p}", "IDENT"),
                     writes=(f"pb{tb}",))
            dst = dst_handle.ap()[:, :].rearrange("p (g t) -> p g t", t=T)[
                :, :, 128 * ti:128 * (ti + 1)]
            em.op("vector", lambda e, d=dst,
                  s=pb[tb].ap()[:, :].rearrange("p (g t) -> p g t", t=128):
                  e.tensor_copy(d, s), reads=(f"pb{tb}",), writes=(dst_key,))

        pend_t = []
        for ti in range(NT):
            p = ti % 2
            kb, kkey = p, f"pb{p}"
            vb, vkey = 2 + p, f"pb{2 + p}"
            proj_group(kb, kkey, ti, wq_bf, tuple(f"wqbf{i}" for i in range(ND)))
            proj_group(vb, vkey, ti, gsil, tuple(f"wvbf{i}" for i in range(ND)))
            if pend_t:
                pend_t.pop(0)()
            s0 = 32 * p
            rms4(pb[kb].ap(), kkey, p, f"krs{p}")
            kn = qf.ap()[:, 512 * p:512 * (p + 1)].rearrange(
                "p (h e) -> p h e", e=128)
            em.op("vector", lambda e, o=kn,
                  i_=pb[kb].ap()[:, :].rearrange("p (h e) -> p h e", e=128),
                  r=col(s0 + 17, 4).unsqueeze(2).broadcast_to((128, 4, 128)):
                  e.tensor_tensor(o, i_, r, op=ALU.mult),
                  reads=(kkey, f"krs{p}"), writes=(f"qf{p}",))
            kro = sqs.ap()[:, 512 * p:512 * (p + 1)].rearrange(
                "p (h e) -> p h e", e=128)
            rope_emit(kn, kro, ti, (f"qf{p}",), (f"sqs{p}",))
            em.op("scalar", lambda e, o=kqbf.ap()[:, 512 * p:512 * (p + 1)],
                  i_=sqs.ap()[:, 512 * p:512 * (p + 1)]: e.copy(o, i_),
                  reads=(f"sqs{p}",), writes=(f"kqbf{p}",))
            pend_t.append(lambda ti=ti: emit_kq_transpose(ti, kT, "kT"))
            em.op("scalar", lambda e, o=v_st.ap()[:, 512 * ti:512 * (ti + 1)],
                  i_=pb[vb].ap()[:, :]: e.copy(o, i_),
                  reads=(vkey,), writes=("v_st",))
        while pend_t:
            pend_t.pop(0)()

        if "kv" in dbg:
            dkT = nc.dram_tensor("dbg_kT", [128, GK * T], BF16,
                                 kind="ExternalOutput").ap()
            em.dma("gpsimd", "sdbg", dkT[:, :], kT.ap()[:, :], reads=("kT",))
            dv = nc.dram_tensor("dbg_v", [128, NT * 512], BF16,
                                kind="ExternalOutput").ap()
            em.dma("gpsimd", "sdbg", dv[:, :], v_st.ap()[:, :], reads=("v_st",))

        for ti in range(NT):
            em.overlay(f"gsil{ti}", [f"wvbf{ti}"])

        for g in range(GK):
            for i in range(ND):
                em.dma("gpsimd", "swq", wq_bf.ap()[:, 512 * i:512 * (i + 1)],
                       wq_in[128 * i:128 * (i + 1), 512 * g:512 * (g + 1)],
                       writes=(f"wqbf{i}",))
            for i in range(ND):
                em.res[f"wqbf{i}"]["w"]["swq"] = em.sems["swq"].n
            pend_t = []
            for ti in range(NT):
                p = ti % 2
                qb, qkey = p, f"pb{p}"
                proj_group(qb, qkey, ti, wq_bf,
                           tuple(f"wqbf{i}" for i in range(ND)))
                if pend_t:
                    pend_t.pop(0)()
                s0 = 32 * p
                rms4(pb[qb].ap(), qkey, p, f"krs{p}")
                em.op("vector", lambda e, o=col(s0 + 21, 4),
                      i_=SCAL.ap()[:, 4 * g:4 * (g + 1)],
                      s=LOGP.ap()[:, ti:ti + 1]:
                      e.tensor_scalar(o, i_, s, None, op0=ALU.mult),
                      reads=("SCAL", "LOGP"), writes=(f"lsc{p}",))
                em.op("vector", lambda e, a_=col(s0 + 17, 4),
                      b_=col(s0 + 21, 4), o=col(s0 + 25, 4):
                      e.tensor_mul(o, a_, b_),
                      reads=(f"krs{p}", f"lsc{p}"), writes=(f"qrall{p}",))
                qn = qf.ap()[:, 512 * p:512 * (p + 1)].rearrange(
                    "p (h e) -> p h e", e=128)
                em.op("vector", lambda e, o=qn,
                      i_=pb[qb].ap()[:, :].rearrange("p (h e) -> p h e", e=128),
                      r=col(s0 + 25, 4).unsqueeze(2).broadcast_to((128, 4, 128)):
                      e.tensor_tensor(o, i_, r, op=ALU.mult),
                      reads=(qkey, f"qrall{p}"), writes=(f"qf{p}",))
                qro = sqs.ap()[:, 512 * p:512 * (p + 1)].rearrange(
                    "p (h e) -> p h e", e=128)
                rope_emit(qn, qro, ti, (f"qf{p}",), (f"sqs{p}",))
                em.op("scalar", lambda e, o=kqbf.ap()[:, 512 * p:512 * (p + 1)],
                      i_=sqs.ap()[:, 512 * p:512 * (p + 1)]: e.copy(o, i_),
                      reads=(f"sqs{p}",), writes=(f"kqbf{p}",))
                pend_t.append(lambda ti=ti: emit_kq_transpose(ti, qT, "qT"))
            while pend_t:
                pend_t.pop(0)()
            for i in range(ND):
                em.dma("gpsimd", "swq", wq_bf.ap()[:, 512 * i:512 * (i + 1)],
                       wg_in[128 * i:128 * (i + 1), 512 * g:512 * (g + 1)],
                       writes=(f"wqbf{i}",))
            for i in range(ND):
                em.res[f"wqbf{i}"]["w"]["swq"] = em.sems["swq"].n
            for ti in range(NT):
                p = ti % 2
                gb, gkey = 2 + p, f"pb{2 + p}"
                proj_group(gb, gkey, ti, wq_bf,
                           tuple(f"wqbf{i}" for i in range(ND)))
                em.op("scalar", lambda e,
                      o=gsil.ap()[:, 512 * ti:512 * (ti + 1)],
                      i_=pb[gb].ap()[:, :]: e.activation(o, i_, AF.Silu),
                      reads=(gkey,), writes=(f"gsil{ti}",))
            def unit_params(u):
                ti, h = u // 4, u % 4
                Wt = min(ti, 8) + 1
                jlo = max(0, ti - 8)
                win = 128 * Wt
                nch = (win + 511) // 512
                hp = h % 2
                return ti, h, Wt, jlo, win, nch, hp

            def emitA(u):
                ti, h, Wt, jlo, win, nch, hp = unit_params(u)
                sbanks = [3 * hp + c for c in range(nch)]
                span0 = 512 * 3 * hp
                fns = []
                for c in range(nch):
                    c0 = 512 * c
                    w = min(512, win - c0)
                    fns.append(
                        lambda e, o=pb[sbanks[c]].ap()[:, 0:w],
                        l=qT.ap()[:, 2048 * h + 128 * ti:
                                  2048 * h + 128 * ti + 128],
                        r=kT.ap()[:, 2048 * g + 128 * jlo + c0:
                                  2048 * g + 128 * jlo + c0 + w]:
                        e.matmul(o, l, r, start=True, stop=False,
                                 skip_group_check=True))
                if ti >= 8:
                    fns.append(lambda e, o=PBIG.ap()[:, span0:span0 + 128]:
                               e.matmul(o, IDENT.ap()[:, :], TRIU.ap()[:, :],
                                        start=False, stop=False,
                                        skip_group_check=True))
                fns.append(lambda e,
                           o=PBIG.ap()[:, span0 + win - 128:span0 + win]:
                           e.matmul(o, IDENT.ap()[:, :], TRIL.ap()[:, :],
                                    start=False, stop=True,
                                    skip_group_check=True))
                em.group("tensor", fns, reads=("qT", "kT", "IDENT", "TRIU",
                                               "TRIL"),
                         writes=tuple(f"pb{b_}" for b_ in sbanks))

            def emitB(u):
                ti, h, Wt, jlo, win, nch, hp = unit_params(u)
                sbanks = [3 * hp + c for c in range(nch)]
                skeys = tuple(f"pb{b_}" for b_ in sbanks)
                span = PBIG.ap()[:, 512 * 3 * hp:512 * 3 * hp + win]
                s0 = 64 + 16 * hp
                em.op("vector", lambda e, o=col(s0 + 3), i_=span:
                      e.reduce_max(o, i_, axis=AX.X, negate=True),
                      reads=skeys, writes=(f"nm{hp}",))
                em.op("scalar", lambda e,
                      o=p_bf.ap()[:, 1280 * hp:1280 * hp + win],
                      i_=span, acc=col(s0 + 8), bi=col(s0 + 3):
                      e.activation(o, i_, AF.Exp, bias=bi,
                                   accum_out=acc),
                      reads=skeys + (f"nm{hp}",),
                      writes=(f"p_bf{hp}", f"p_bf{hp}b", f"l{hp}"))
                em.op("vector", lambda e, o=col(s0 + 9 + h),
                      i_=col(s0 + 8): e.reciprocal(o, i_),
                      reads=(f"l{hp}",), writes=(f"rl_{h}",))

            def emitC(u):
                ti, h, Wt, jlo, win, nch, hp = unit_params(u)
                ykey = f"pb{6 + hp}"
                nb = (Wt + 3) // 4

                def pT_batch(bt):
                    ntile = min(4, Wt - 4 * bt)
                    tbk = 3 * hp + (bt % 3)
                    fns = [lambda e,
                           o=pb[tbk].ap()[:, 128 * k:128 * (k + 1)],
                           l=p_bf.ap()[:, 1280 * hp + 128 * (4 * bt + k):
                                       1280 * hp + 128 * (4 * bt + k + 1)]:
                           e.matmul(o, l, IDENT.ap()[:, :], start=True,
                                    stop=True, skip_group_check=True)
                           for k in range(ntile)]
                    pkeys = ((f"p_bf{hp}",) if 4 * bt + ntile <= 4
                             else (f"p_bf{hp}", f"p_bf{hp}b"))
                    em.group("tensor", fns, reads=pkeys + ("IDENT",),
                             writes=(f"pb{tbk}",))

                def evac_batch(bt):
                    ntile = min(4, Wt - 4 * bt)
                    tbk = 3 * hp + (bt % 3)
                    pt = bt % 2
                    em.op("vector", lambda e,
                          o=pT_bf.ap()[:, 512 * pt:512 * pt + 128 * ntile],
                          i_=pb[tbk].ap()[:, 0:128 * ntile]:
                          e.tensor_copy(o, i_), reads=(f"pb{tbk}",),
                          writes=(f"pT{pt}",))

                def pv_batch(bt):
                    ntile = min(4, Wt - 4 * bt)
                    pt = bt % 2
                    fns = [lambda e, o=pb[6 + hp].ap()[:, 0:128],
                           l=pT_bf.ap()[:, 512 * pt + 128 * k:
                                        512 * pt + 128 * (k + 1)],
                           r=v_st.ap()[:, 512 * (jlo + 4 * bt + k) + 128 * g:
                                       512 * (jlo + 4 * bt + k) + 128 * (g + 1)],
                           s=(bt == 0 and k == 0),
                           st=(4 * bt + k == Wt - 1):
                           e.matmul(o, l, r, start=s, stop=st,
                                    skip_group_check=True)
                           for k in range(ntile)]
                    em.group("tensor", fns, reads=(f"pT{pt}", "v_st"),
                             writes=(ykey,))

                pT_batch(0)
                for bt in range(nb):
                    if bt + 1 < nb:
                        pT_batch(bt + 1)
                    evac_batch(bt)
                    pv_batch(bt)

            def emitD(u):
                ti, h, Wt, jlo, win, nch, hp = unit_params(u)
                ykey = f"pb{6 + hp}"
                em.op("vector", lambda e,
                      o=w_all.ap()[:, 128 * h:128 * (h + 1)],
                      i0=pb[6 + hp].ap()[:, 0:128],
                      sc=col(64 + 16 * hp + 9 + h),
                      i1=gsil.ap()[:, 512 * ti + 128 * h:
                                   512 * ti + 128 * (h + 1)]:
                      e.scalar_tensor_tensor(o, i0, sc, i1, op0=ALU.mult,
                                             op1=ALU.mult),
                      reads=(ykey, f"rl_{h}", f"gsil{ti}"),
                      writes=("w_all",))
                em.op("scalar", lambda e,
                      o=sqs.ap()[:, 128 * h:128 * (h + 1)],
                      i_=w_all.ap()[:, 128 * h:128 * (h + 1)],
                      acc=col(96 + h): e.activation(o, i_, AF.Square,
                                                    accum_out=acc),
                      reads=("w_all",), writes=("sqs0", "wss"))
                if h != 3:
                    return
                em.op("scalar", lambda e, i_=col(96, 4), o=col(100, 4):
                      e.activation(o, i_, AF.Ln, scale=1.0 / DH, bias=EPS6),
                      reads=("wss", "CEPS"), writes=("wsd",))
                em.op("scalar", lambda e, i_=col(100, 4), o=col(104, 4):
                      e.activation(o, i_, AF.Exp, scale=-0.5),
                      reads=("wsd",), writes=("rw",))
                em.op("vector", lambda e,
                      o=w_all2.ap()[:, :].rearrange("p (h e) -> p h e", e=128),
                      i_=w_all.ap()[:, :].rearrange("p (h e) -> p h e", e=128),
                      r=col(104, 4).unsqueeze(2).broadcast_to((128, 4, 128)):
                      e.tensor_tensor(o, i_, r, op=ALU.mult),
                      reads=("w_all", "rw"), writes=("w_all2",))
                sp = ti % 2
                em.op("vector", lambda e,
                      o=sstage.ap()[:, 512 * sp:512 * (sp + 1)].rearrange(
                          "p (h e) -> p h e", e=128),
                      i_=w_all2.ap()[:, :].rearrange("p (h e) -> p h e", e=128),
                      r=GN.ap()[:, :].unsqueeze(1).broadcast_to((128, 4, 128)):
                      e.tensor_tensor(o, i_, r, op=ALU.mult),
                      reads=("w_all2", "GN"), writes=(f"sstage{sp}",))
                em.dma("gpsimd", "s_sout",
                       s_scr[128 * ti:128 * (ti + 1), 512 * g:512 * (g + 1)],
                       sstage.ap()[:, 512 * sp:512 * (sp + 1)],
                       reads=(f"sstage{sp}",), writes=("s_scr",))

            NU = NT * 4
            emitA(0)
            emitB(0)
            for u in range(1, NU):
                emitA(u)
                emitC(u - 1)
                emitB(u)
                emitD(u - 1)
            emitC(NU - 1)
            emitD(NU - 1)

        if "qT" in dbg:
            dqT = nc.dram_tensor("dbg_qT", [128, 4 * T], BF16,
                                 kind="ExternalOutput").ap()
            em.dma("gpsimd", "sdbg", dqT[:, :], qT.ap()[:, :], reads=("qT",))
            dgs = nc.dram_tensor("dbg_gsil", [128, NT * 512], BF16,
                                 kind="ExternalOutput").ap()
            em.dma("gpsimd", "sdbg", dgs[:, :], gsil.ap()[:, :],
                   reads=tuple(f"gsil{t}" for t in range(NT)))

        em.barrier(extra_streams=("s_sout",))
        em.res["s_scr"] = {"w": {"s_sout": em.sems["s_sout"].n}, "r": {}}
        for key in ("IDENT", "HAD", "COS", "SIN", "LOGP", "TRIU", "TRIL",
                    "CEPS", "SCAL", "GN", "xnT"):
            em.res[key] = {"w": {}, "r": {}}

        w2all = []

        def emit_w2pre_load_and_bfly(dt):
            src = w2x if dt % 2 == 0 else w2y
            skey = "w2x" if dt % 2 == 0 else "w2y"
            scr, ckey = w2z, "w2z"
            w2all.append(lambda dt=dt, src=src, skey=skey: em.dma(
                "gpsimd", f"sw2_{dt % 2}", src.ap()[:, :],
                w2_in[128 * dt:128 * (dt + 1), :], writes=(skey,)))
            for s in range(6):
                blk = 128 * (1 << s)
                fr, to = (src, scr) if s % 2 == 0 else (scr, src)
                frk, tok = (skey, ckey) if s % 2 == 0 else (ckey, skey)
                frv = fr.ap()[:, :].rearrange("p (n two b) -> p n two b",
                                              two=2, b=blk)
                tov = to.ap()[:, :].rearrange("p (n two b) -> p n two b",
                                              two=2, b=blk)
                w2all.append(lambda frv=frv, tov=tov, frk=frk, tok=tok:
                             em.op("vector", lambda e, o=tov[:, :, 0, :],
                                   a_=frv[:, :, 0, :], b_=frv[:, :, 1, :]:
                                   e.tensor_add(o, a_, b_),
                                   reads=(frk,), writes=(tok,)))
                w2all.append(lambda frv=frv, tov=tov, frk=frk, tok=tok:
                             em.op("vector", lambda e, o=tov[:, :, 1, :],
                                   a_=frv[:, :, 0, :], b_=frv[:, :, 1, :]:
                                   e.tensor_sub(o, a_, b_),
                                   reads=(frk,), writes=(tok,)))
            for bt in range(16):
                w2all.append(lambda dt=dt, bt=bt, src=src, skey=skey:
                             emit_one_w2batch(dt, bt, src, skey))

        w2h_pend = []

        def emit_w2pre_h(item):
            dt, bt = item
            em.group("tensor",
                     [lambda e, o=pb[7].ap()[:, :], l=HAD.ap()[:, :],
                      r=w2st.ap()[:, 512 * bt:512 * (bt + 1)]:
                      e.matmul(o, l, r, start=True, stop=True,
                               skip_group_check=True)],
                     reads=(f"w2st{bt % 2}", "HAD"), writes=("pb7",))
            em.op("scalar", lambda e, o=w2st.ap()[:, 512 * bt:512 * (bt + 1)],
                  i_=pb[7].ap()[:, :]: e.activation(o, i_, AF.Copy,
                                                    scale=FWHT_SCALE),
                  reads=("pb7",), writes=(f"w2st{bt % 2}",))
            if bt == 15:
                dstv = w2pp[dt // 2].rearrange("p (a e) -> p a e", e=256)[
                    :, :, 128 * (dt % 2):128 * (dt % 2) + 128]
                srcv = w2st.ap()[:, :].rearrange("p (a e) -> p a e", e=128)
                for q4 in range(4):
                    em.dma("gpsimd", "sw2pp",
                           dstv[:, 16 * q4:16 * (q4 + 1), :],
                           srcv[:, 16 * q4:16 * (q4 + 1), :],
                           reads=("w2st0", "w2st1"), writes=("w2pp",))

        def emit_one_w2batch(dt, bt, src, skey):
            fns = [lambda e, o=pb[6].ap()[:, 128 * k:128 * (k + 1)],
                   l=src.ap()[:, 128 * (4 * bt + k):128 * (4 * bt + k + 1)]:
                   e.matmul(o, l, IDENT.ap()[:, :], start=True, stop=True,
                            skip_group_check=True) for k in range(4)]
            em.group("tensor", fns, reads=(skey, "IDENT"), writes=("pb6",))
            em.op("vector", lambda e, o=w2st.ap()[:, 512 * bt:512 * (bt + 1)],
                  i_=pb[6].ap()[:, :]: e.tensor_copy(o, i_),
                  reads=("pb6",), writes=(f"w2st{bt % 2}",))
            if w2h_pend:
                emit_w2pre_h(w2h_pend.pop(0))
            w2h_pend.append((dt, bt))
            if bt == 15:
                while w2h_pend:
                    emit_w2pre_h(w2h_pend.pop(0))

        def emit_wt(aa, nat, matT, natkey, tkey, bank, batch):
            sa_, pa = aa % 4, aa % 2
            fns = [lambda e, o=pb[bank].ap()[:, 128 * k:128 * (k + 1)],
                   l=nat.ap()[:, sa_ * D + 128 * (4 * batch + k):
                              sa_ * D + 128 * (4 * batch + k + 1)]:
                   e.matmul(o, l, IDENT.ap()[:, :], start=True, stop=True,
                            skip_group_check=True) for k in range(4)]
            em.group("tensor", fns, reads=(f"{natkey}{sa_}", "IDENT"),
                     writes=(f"pb{bank}",))
            em.op("scalar", lambda e,
                  o=matT.ap()[:, pa * D + 512 * batch:pa * D + 512 * (batch + 1)],
                  i_=pb[bank].ap()[:, :]: e.copy(o, i_),
                  reads=(f"pb{bank}",), writes=(f"{tkey}{pa}",))

        def load_w13(aa):
            sa_ = aa % 4
            em.dma("gpsimd", f"sw1_{sa_}", w1nat.ap()[:, sa_ * D:(sa_ + 1) * D],
                   w1_in[128 * aa:128 * (aa + 1), :], writes=(f"w1nat{sa_}",))
            em.dma("gpsimd", f"sw3_{sa_}", w3nat.ap()[:, sa_ * D:(sa_ + 1) * D],
                   w3_in[128 * aa:128 * (aa + 1), :], writes=(f"w3nat{sa_}",))

        load_w13(0)
        load_w13(1)
        for aa in range(NA):
            pa = aa % 2
            if aa + 2 < NA:
                load_w13(aa + 2)
            if aa % 3 == 0 and aa // 3 < ND:
                emit_w2pre_load_and_bfly(aa // 3)
            if aa == 0:
                for bt in range(4):
                    emit_wt(0, w1nat, w1T, "w1nat", "w1T", 6, bt)
                    emit_wt(0, w3nat, w3T, "w3nat", "w3T", 7, bt)
            for c in range(4):
                ub, ukey = c % 2, f"pb{c % 2}"
                fns = [lambda e, o=pb[ub].ap()[:, :],
                       l=w1T.ap()[:, pa * D + 128 * i:pa * D + 128 * (i + 1)],
                       r=xnT_c(i, c), s=(i == 0), st=(i == ND - 1):
                       e.matmul(o, l, r, start=s, stop=st,
                                skip_group_check=True) for i in range(ND)]
                em.group("tensor", fns, reads=(f"w1T{pa}", "xnT"),
                         writes=(ukey,))
                em.op("scalar", lambda e,
                      o=sg_scr.ap()[:, 512 * (c % 2):512 * (c % 2 + 1)],
                      i_=pb[ub].ap()[:, :]: e.activation(o, i_, AF.Silu),
                      reads=(ukey,), writes=(f"sg{c % 2}",))
                if aa + 1 < NA:
                    emit_wt(aa + 1, w1nat, w1T, "w1nat", "w1T", 6, c)
                lb, lkey = 2 + c % 2, f"pb{2 + c % 2}"
                fns = [lambda e, o=pb[lb].ap()[:, :],
                       l=w3T.ap()[:, pa * D + 128 * i:pa * D + 128 * (i + 1)],
                       r=xnT_c(i, c), s=(i == 0), st=(i == ND - 1):
                       e.matmul(o, l, r, start=s, stop=st,
                                skip_group_check=True) for i in range(ND)]
                em.group("tensor", fns, reads=(f"w3T{pa}", "xnT"),
                         writes=(lkey,))
                em.op("vector", lambda e,
                      o=hidb.ap()[:, pa * D + 512 * c:pa * D + 512 * (c + 1)],
                      a_=sg_scr.ap()[:, 512 * (c % 2):512 * (c % 2 + 1)],
                      b_=pb[lb].ap()[:, :]: e.tensor_mul(o, a_, b_),
                      reads=(f"sg{c % 2}", lkey), writes=(f"hidb{pa}",))
                if aa + 1 < NA:
                    emit_wt(aa + 1, w3nat, w3T, "w3nat", "w3T", 7, c)
                em.dma("gpsimd", "shid_out", hid_scr[c, aa][:, :],
                       hidb.ap()[:, pa * D + 512 * c:pa * D + 512 * (c + 1)],
                       reads=(f"hidb{pa}",), writes=("hid_scr",))
                for _ in range(2):
                    if w2all:
                        w2all.pop(0)()
        while w2all:
            w2all.pop(0)()
        while w2h_pend:
            emit_w2pre_h(w2h_pend.pop(0))

        em.barrier(extra_streams=("shid_out", "sw2pp", "s_sout"))
        for nm, src, shape in (("s", s_scr, [NT * 128, D]),
                               ("hid", hid_scr, [4, NA, 128, 512])):
            if nm in dbg:
                dd = nc.dram_tensor(f"dbg_{nm}", shape, BF16,
                                    kind="ExternalOutput").ap()
                key = {"s": "s_scr", "hid": "hid_scr", "w2pp": "w2pp"}[nm]
                em.dma("gpsimd", "sdbg", dd[:, :] if len(shape) == 2
                       else dd[:, :, :, :], src[:, :] if len(shape) == 2
                       else src[:, :, :, :], reads=(key,))
        em.res["hid_scr"] = {"w": {"shid_out": em.sems["shid_out"].n}, "r": {}}
        em.res["w2pp"] = {"w": {"sw2pp": em.sems["sw2pp"].n}, "r": {}}
        em.res["s_scr"] = {"w": {"s_sout": em.sems["s_sout"].n}, "r": {}}

        for c in range(4):
            for dc0 in range(2):
                w2cb0 = w2c0 if dc0 == 0 else w2c1
                for sl in range(4):
                    seg = HF // 2
                    em.dma("sync", f"sw2c{dc0}",
                           w2cb0.ap()[:, seg * sl:seg * (sl + 1)],
                           w2pp[dc0][:, seg * sl:seg * (sl + 1)],
                           reads=("w2pp",), writes=(f"w2cb{dc0}",))
            for k in range(8):
                em.dma("sync", f"shid_in{k // 2}",
                       hidc.ap()[:, :].rearrange("p (a e) -> p a e", e=512)[
                           :, 8 * k:8 * (k + 1), :],
                       hid_scr[c, 8 * k:8 * (k + 1)].transpose([1, 0, 2]),
                       reads=("hid_scr",), writes=(f"hidcQ{k // 2}",))
            for t1 in range(4):
                ti = 4 * c + t1
                em.dma("sync", f"sxr{t1}",
                       outstage.ap()[:, D * t1:D * (t1 + 1)],
                       x_in[128 * ti:128 * (ti + 1), :], writes=(f"os{t1}",))
            for dt in range(ND):
                em.dma("sync", "ss_in",
                       sTb.ap()[:, 512 * dt:512 * (dt + 1)],
                       s_scr[512 * c:512 * (c + 1),
                             128 * dt:128 * (dt + 1)],
                       reads=("s_scr",), writes=("sTb",), transpose=True)
            for dc in range(8):
                pd = dc % 2
                w2cb = w2c0 if pd == 0 else w2c1
                w2key = f"w2cb{pd}"
                if dc >= 2:
                    for sl in range(4):
                        seg = HF // 2
                        em.dma("sync", f"sw2c{pd}",
                               w2cb.ap()[:, seg * sl:seg * (sl + 1)],
                               w2pp[dc][:, seg * sl:seg * (sl + 1)],
                               reads=("w2pp",), writes=(w2key,))
                for db in range(2):
                    dt_g = 2 * dc + db
                    bank = 2 * pd + db
                    bkey = f"pb{bank}"
                    for q in range(4):
                        fns = [lambda e, o=pb[bank].ap()[:, :],
                               l=w2cb.ap()[:, 256 * a2 + 128 * db:
                                           256 * a2 + 128 * (db + 1)],
                               r=hidc.ap()[:, 512 * a2:512 * (a2 + 1)],
                               s=(a2 == 0), st=(a2 == NA - 1):
                               e.matmul(o, l, r, start=s, stop=st,
                                        skip_group_check=True)
                               for a2 in range(16 * q, 16 * (q + 1))]
                        em.group("tensor", fns,
                                 reads=(f"hidcQ{q}", w2key), writes=(bkey,))
                    gp = dt_g % 2
                    em.op("scalar", lambda e,
                          o=gfb.ap()[:, 512 * gp:512 * (gp + 1)],
                          i_=pb[bank].ap()[:, :]:
                          e.activation(o, i_, AF.Silu),
                          reads=(bkey,), writes=(f"gf{gp}",))
                    em.op("vector", lambda e,
                          o=wbf.ap()[:, 512 * gp:512 * (gp + 1)],
                          a_=gfb.ap()[:, 512 * gp:512 * (gp + 1)],
                          b_=sTb.ap()[:, 512 * dt_g:512 * (dt_g + 1)]:
                          e.tensor_mul(o, a_, b_),
                          reads=(f"gf{gp}", "sTb"), writes=(f"wbf{gp}",))
                    tbank = 4 + gp
                    fns = [lambda e,
                           o=pb[tbank].ap()[:, 128 * k:128 * (k + 1)],
                           l=wbf.ap()[:, 512 * gp + 128 * k:
                                      512 * gp + 128 * (k + 1)]:
                           e.matmul(o, l, IDENT.ap()[:, :], start=True,
                                    stop=True, skip_group_check=True)
                           for k in range(4)]
                    em.group("tensor", fns, reads=(f"wbf{gp}", "IDENT"),
                             writes=(f"pb{tbank}",))
                    for k in range(4):
                        em.op("vector", lambda e,
                              o=outstage.ap()[:, D * k + 128 * dt_g:
                                              D * k + 128 * (dt_g + 1)],
                              i_=pb[tbank].ap()[:, 128 * k:128 * (k + 1)]:
                              e.tensor_add(o, o, i_),
                              reads=(f"pb{tbank}", f"os{k}"),
                              writes=(f"os{k}",))
            for t1 in range(4):
                ti = 4 * c + t1
                em.dma("scalar", f"sout{t1}",
                       out_d[128 * ti:128 * (ti + 1), :],
                       outstage.ap()[:, D * t1:D * (t1 + 1)],
                       reads=(f"os{t1}",))

        for t1 in range(4):
            em.wait("sync", f"sout{t1}", em.sems[f"sout{t1}"].n)

        with nc.Block() as block:
            @block.sync
            def _(e):
                for f in em.ops["sync"]:
                    f(e)

            @block.gpsimd
            def _(e):
                for f in em.ops["gpsimd"]:
                    f(e)

            @block.vector
            def _(e):
                for f in em.ops["vector"]:
                    f(e)

            @block.scalar
            def _(e):
                for f in em.ops["scalar"]:
                    f(e)

            @block.tensor
            def _(e):
                for f in em.ops["tensor"]:
                    f(e)

    return nc


def kernel(L0_attn_hidden_states, L0_attn_Wq, L0_attn_Wk, L0_attn_Wv,
           L0_attn_Wg, L0_attn_softmax_scaler, L0_attn_gn_weight,
           L0_ffn_W1, L0_ffn_W2, L0_ffn_W3):
    x = np.ascontiguousarray(np.asarray(L0_attn_hidden_states, np.float32))
    wq = np.ascontiguousarray(np.asarray(L0_attn_Wq, np.float32))
    wk = np.ascontiguousarray(np.asarray(L0_attn_Wk, np.float32))
    wv = np.ascontiguousarray(np.asarray(L0_attn_Wv, np.float32))
    wg = np.ascontiguousarray(np.asarray(L0_attn_Wg, np.float32))
    w1 = np.ascontiguousarray(np.asarray(L0_ffn_W1, np.float32))
    w2 = np.ascontiguousarray(np.asarray(L0_ffn_W2, np.float32))
    w3 = np.ascontiguousarray(np.asarray(L0_ffn_W3, np.float32))
    scal = np.ascontiguousarray(np.broadcast_to(
        np.asarray(L0_attn_softmax_scaler, np.float32)[None, :], (128, H)))
    gn = np.ascontiguousarray(np.broadcast_to(
        np.asarray(L0_attn_gn_weight, np.float32)[None, :], (128, DH)))

    nc = build_program()
    in_maps = [{
        "x": np.ascontiguousarray(x[b]),
        "wq": wq, "wk": wk, "wv": wv, "wg": wg,
        "scaler_rep": scal, "gn_rep": gn,
        "w1": w1, "w2": w2, "w3": w3,
    } for b in range(8)]
    res = run_bass_kernel_spmd(nc, in_maps, core_ids=list(range(8)))
    out = np.stack([res.results[b]["out"] for b in range(8)], axis=0)
    return out.astype(np.float32)


if __name__ == "__main__":
    nc = build_program()
    print("build ok; instructions:", nc.n_instructions())


# version 35 (final)
# speedup vs baseline: 1.0988x; 1.0981x over previous
import math
import contextlib
import numpy as np
import ml_dtypes
import concourse.bass as bass
import concourse.mybir as mybir
import concourse.bass_utils as _bu
from concourse.bass_utils import run_bass_kernel_spmd

if not hasattr(_bu, "_orig_run_command"):
    _bu._orig_run_command = _bu.run_command


def _run_command_ldwopt(argv, **kwargs):
    argv = ["--enable-ldw-opt=true" if a == "--enable-ldw-opt=false" else a
            for a in argv]
    return _bu._orig_run_command(argv, **kwargs)


_bu.run_command = _run_command_ldwopt

F32 = mybir.dt.float32
BF16 = mybir.dt.bfloat16
AF = mybir.ActivationFunctionType
ALU = mybir.AluOpType
AX = mybir.AxisListType

T = 2048
D = 2048
NT = 16
ND = 16
H = 16
GK = 4
DH = 128
HF = 8192
NA = 64
INV_SQRT_DH = 1.0 / math.sqrt(DH)
FWHT_SCALE = 1.0 / math.sqrt(HF)
NEG = -1e30
BF = ml_dtypes.bfloat16


def _consts():
    i = np.arange(128)
    pop = np.zeros((128, 128), np.int64)
    for r in range(128):
        pop[r] = [bin(r & j).count("1") & 1 for j in i]
    had = (1.0 - 2.0 * pop).astype(np.float32)
    ident = np.eye(128, dtype=np.float32)
    inv_freq = 10000.0 ** (-np.arange(0, 128, 2, np.float64) / 128.0)
    fr = np.outer(np.arange(T, dtype=np.float64), inv_freq)
    cosl = np.cos(fr).astype(np.float32).reshape(NT, 128, 64).transpose(
        1, 0, 2).reshape(128, NT * 64).copy()
    sinl = np.sin(fr).astype(np.float32).reshape(NT, 128, 64).transpose(
        1, 0, 2).reshape(128, NT * 64).copy()
    logp = (np.log(np.arange(1, T + 1, dtype=np.float64)) / np.sqrt(128.0)
            ).astype(np.float32).reshape(NT, 128).T.copy()
    ii = np.arange(128)[:, None]
    jj = np.arange(128)[None, :]
    triu = np.where(jj >= ii, 0.0, NEG).astype(np.float32)
    tril = np.where(jj <= ii, 0.0, NEG).astype(np.float32)
    ceps = np.zeros((128, 8), np.float32)
    ceps[:, 0] = 1e-6
    ceps[:, 1] = 1e-5
    ceps[:, 2] = 1.0
    return dict(had=had.astype(BF), ident=ident.astype(BF), cos=cosl, sin=sinl,
                logp=logp, triu=triu.astype(BF), tril=tril.astype(BF),
                ceps=ceps)


class Sem:
    def __init__(self, ctx, nc, name):
        self.h = ctx.enter_context(nc.semaphore(name))
        self.n = 0
        self.name = name


class Em:

    ENGINES = ("tensor", "vector", "scalar", "sync", "gpsimd")

    def __init__(self, ctx, nc):
        self.nc = nc
        self.ctx = ctx
        self.ops = {e: [] for e in self.ENGINES}
        self.prog = {e: Sem(ctx, nc, f"pg_{e}") for e in self.ENGINES}
        self.res = {}
        self.cover = {e: {} for e in self.ENGINES}
        self.sems = {}
        self.nwaits = 0

    def stream_sem(self, name):
        if name not in self.sems:
            self.sems[name] = Sem(self.ctx, self.nc, name)
        return self.sems[name]

    def _handle(self, name):
        if name in self.sems:
            return self.sems[name].h
        for e in self.ENGINES:
            if self.prog[e].name == name:
                return self.prog[e].h
        raise KeyError(name)

    def _needs(self, eng, reads, writes):
        need = {}

        def add(sn, v):
            if self.cover[eng].get(sn, -1) >= v:
                return
            if need.get(sn, -1) < v:
                need[sn] = v

        for k in reads:
            st = self.res.get(k)
            if st:
                for sn, v in st["w"].items():
                    add(sn, v)
        for k in writes:
            st = self.res.get(k)
            if st:
                for sn, v in st["w"].items():
                    add(sn, v)
                for sn, v in st["r"].items():
                    add(sn, v)
        return need

    def _commit(self, eng, need, semname, semval, reads, writes):
        for sn, v in need.items():
            if self.cover[eng].get(sn, -1) < v:
                self.cover[eng][sn] = v
        for k in writes:
            self.res[k] = {"w": {semname: semval}, "r": {}}
        for k in reads:
            st = self.res.setdefault(k, {"w": {}, "r": {}})
            if st["r"].get(semname, -1) < semval:
                st["r"][semname] = semval

    def _mk_emit(self, fns, need):
        handles = {n: self._handle(n) for n in need}
        self.nwaits += len(need)

        def emit(e, fns=fns, need=dict(need), handles=handles):
            for n, v in need.items():
                e.wait_ge(handles[n], v)
            last = None
            for f in fns:
                last = f(e)
            return last

        return emit

    def op(self, eng, fn, reads=(), writes=()):
        self.group(eng, [fn], reads, writes)

    def group(self, eng, fns, reads=(), writes=()):
        need = self._needs(eng, reads, writes)
        sem = self.prog[eng]
        sem.n += 1
        inner = self._mk_emit(fns, need)

        def emit(e, inner=inner, sem=sem):
            inner(e).then_inc(sem.h, 1)

        self.ops[eng].append(emit)
        self._commit(eng, need, sem.name, sem.n, reads, writes)

    def dma(self, eng, stream, out, in_, reads=(), writes=(), transpose=False):
        ssem = self.stream_sem(stream)
        need = self._needs(eng, reads, writes)
        ssem.n += 16
        inner = self._mk_emit(
            [lambda e, out=out, in_=in_, tr=transpose:
             e.dma_start(out, in_, transpose=tr)], need)

        def emit(e, inner=inner, ssem=ssem):
            inner(e).then_inc(ssem.h, 16)

        self.ops[eng].append(emit)
        for sn, v in need.items():
            if self.cover[eng].get(sn, -1) < v:
                self.cover[eng][sn] = v
        for k in writes:
            self.res[k] = {"w": {stream: ssem.n}, "r": {}}
        for k in reads:
            st = self.res.setdefault(k, {"w": {}, "r": {}})
            if st["r"].get(stream, -1) < ssem.n:
                st["r"][stream] = ssem.n

    def wait(self, eng, semname, val):
        h = self._handle(semname)
        self.ops[eng].append(lambda e, h=h, val=val: e.wait_ge(h, val))
        if self.cover[eng].get(semname, -1) < val:
            self.cover[eng][semname] = val

    def overlay(self, newkey, oldkeys):
        st = {"w": {}, "r": {}}
        for k in oldkeys:
            o = self.res.pop(k, None)
            if not o:
                continue
            for sn, v in o["w"].items():
                st["r"][sn] = max(st["r"].get(sn, -1), v)
            for sn, v in o["r"].items():
                st["r"][sn] = max(st["r"].get(sn, -1), v)
        self.res[newkey] = st

    def barrier(self, extra_streams=()):
        targets = [(self.prog[e].name, self.prog[e].n)
                   for e in ("tensor", "vector", "scalar")]
        targets += [(s, self.sems[s].n) for s in extra_streams if s in self.sems]
        for e in self.ENGINES:
            for sn, v in targets:
                if v > 0 and self.cover[e].get(sn, -1) < v:
                    self.wait(e, sn, v)
        self.res = {}


def build_program(dbg=()):
    C = _consts()
    nc = bass.Bass("TRN2", target_bir_lowering=False)

    x_in = nc.dram_tensor("x", [T, D], F32, kind="ExternalInput").ap()
    wq_in = nc.dram_tensor("wq", [D, H * DH], F32, kind="ExternalInput").ap()
    wk_in = nc.dram_tensor("wk", [D, GK * DH], F32, kind="ExternalInput").ap()
    wv_in = nc.dram_tensor("wv", [D, GK * DH], F32, kind="ExternalInput").ap()
    wg_in = nc.dram_tensor("wg", [D, H * DH], F32, kind="ExternalInput").ap()
    scal_in = nc.dram_tensor("scaler_rep", [128, H], F32, kind="ExternalInput").ap()
    gn_in = nc.dram_tensor("gn_rep", [128, DH], F32, kind="ExternalInput").ap()
    w1_in = nc.dram_tensor("w1", [HF, D], F32, kind="ExternalInput").ap()
    w2_in = nc.dram_tensor("w2", [D, HF], F32, kind="ExternalInput").ap()
    w3_in = nc.dram_tensor("w3", [HF, D], F32, kind="ExternalInput").ap()
    out_d = nc.dram_tensor("out", [T, D], F32, kind="ExternalOutput").ap()

    s_scr = nc.dram_tensor("s_scr", [NT * 128, D], BF16, kind="Internal").ap()
    hid_scr = nc.dram_tensor("hid_scr", [4, NA, 128, 512], BF16,
                             kind="Internal").ap()
    w2pp = nc.dram_tensor("w2pp", [8, 128, 2 * HF], BF16, kind="Internal").ap()

    cin = {}
    for nm, arr in (("ident", C["ident"]), ("had", C["had"]), ("cos", C["cos"]),
                    ("sin", C["sin"]), ("logp", C["logp"]), ("triu", C["triu"]),
                    ("tril", C["tril"]), ("ceps", C["ceps"])):
        cin[nm] = nc.inline_tensor(arr, name=f"c_{nm}").ap()

    dbg_outs = {}

    ctx = contextlib.ExitStack()
    with ctx:
        em = Em(ctx, nc)

        def sball(name, cols, dtype, offset):
            return nc.alloc_sbuf_tensor_at(name, [128, cols], dtype,
                                           offset=offset)

        off = int(nc._sbuf_addr_for_side("left"))

        def nxt(cols, dtype, align=64):
            nonlocal off
            esz = 2 if dtype == BF16 else 4
            off = (off + align - 1) // align * align
            o = off
            off += cols * esz
            return o

        IDENT = sball("IDENT", 128, BF16, nxt(128, BF16))
        HAD = sball("HAD", 128, BF16, nxt(128, BF16))
        COS = sball("COS", NT * 64, F32, nxt(NT * 64, F32))
        SIN = sball("SIN", NT * 64, F32, nxt(NT * 64, F32))
        LOGP = sball("LOGP", NT, F32, nxt(NT, F32))
        TRIU = sball("TRIU", 128, BF16, nxt(128, BF16))
        TRIL = sball("TRIL", 128, BF16, nxt(128, BF16))
        CEPS = sball("CEPS", 8, F32, nxt(8, F32))
        SCAL = sball("SCAL", H, F32, nxt(H, F32))
        GN = sball("GN", DH, F32, nxt(DH, F32))
        ST = sball("ST", 128, F32, nxt(128, F32))
        xnT_off = nxt(ND * T, BF16)
        xnT = sball("xnT", ND * T, BF16, xnT_off)
        arena0 = off

        offA = arena0

        def nxtA(cols, dtype, align=64):
            nonlocal offA
            esz = 2 if dtype == BF16 else 4
            offA = (offA + align - 1) // align * align
            o = offA
            offA += cols * esz
            return o

        kT = sball("kT", GK * T, BF16, nxtA(GK * T, BF16))
        v_st = sball("v_st", NT * 512, BF16, nxtA(NT * 512, BF16))
        wq_bf = sball("wq_bf", ND * 512, BF16, nxtA(ND * 512, BF16))
        qT = sball("qT", 4 * T, BF16, nxtA(4 * T, BF16))
        gsil = sball("gsil", NT * 512, BF16, nxtA(NT * 512, BF16))
        p_bf = sball("p_bf", 2 * 1280, BF16, nxtA(2 * 1280, BF16))
        pT_bf = sball("pT_bf", 2 * 512, BF16, nxtA(2 * 512, BF16))
        qf = sball("qf", 2 * 512, F32, nxtA(2 * 512, F32))
        sqs = sball("sqs", 2 * 512, F32, nxtA(2 * 512, F32))
        rtmp = sball("rtmp", 512, F32, nxtA(512, F32))
        w_all = sball("w_all", 512, F32, nxtA(512, F32))
        w_all2 = sball("w_all2", 512, F32, nxtA(512, F32))
        sstage = sball("sstage", 2 * 512, BF16, nxtA(2 * 512, BF16))
        kqbf = sball("kqbf", 2 * 512, BF16, nxtA(2 * 512, BF16))
        endA = offA

        xt = sball("xt", 2 * D, F32, arena0)
        ln_scr = sball("ln_scr", D, F32, arena0 + 16384)
        xnbf = sball("xnbf", 2 * D, BF16, arena0 + 24576)

        offB = arena0

        def nxtB(cols, dtype, align=64):
            nonlocal offB
            esz = 2 if dtype == BF16 else 4
            offB = (offB + align - 1) // align * align
            o = offB
            offB += cols * esz
            return o

        w1nat = sball("w1nat", 4 * D, BF16, nxtB(4 * D, BF16))
        w3nat = sball("w3nat", 4 * D, BF16, nxtB(4 * D, BF16))
        w1T = sball("w1T", 2 * D, BF16, nxtB(2 * D, BF16))
        w3T = sball("w3T", 2 * D, BF16, nxtB(2 * D, BF16))
        hidb = sball("hidb", 2 * D, BF16, nxtB(2 * D, BF16))
        sg_scr = sball("sg_scr", 2 * 512, F32, nxtB(2 * 512, F32))
        w2x = sball("w2x", HF, BF16, nxtB(HF, BF16))
        w2y = sball("w2y", HF, BF16, nxtB(HF, BF16))
        w2z = sball("w2z", HF, BF16, nxtB(HF, BF16))
        w2st = sball("w2st", HF, BF16, nxtB(HF, BF16))
        endB = offB

        offC = xnT_off

        def nxtC(cols, dtype, align=64):
            nonlocal offC
            esz = 2 if dtype == BF16 else 4
            offC = (offC + align - 1) // align * align
            o = offC
            offC += cols * esz
            return o

        hidc = sball("hidc", NA * 512, BF16, nxtC(NA * 512, BF16))
        w2c0 = sball("w2c0", 2 * HF, BF16, nxtC(2 * HF, BF16))
        w2c1 = sball("w2c1", 2 * HF, BF16, nxtC(2 * HF, BF16))
        sTb = sball("sTb", ND * 512, BF16, nxtC(ND * 512, BF16))
        outstage = sball("outstage", 4 * D, F32, nxtC(4 * D, F32))
        gfb = sball("gfb", 2 * 512, F32, nxtC(2 * 512, F32))
        wbf = sball("wbf", 2 * 512, BF16, nxtC(2 * 512, BF16))
        endC = offC

        PBIG = ctx.enter_context(nc.psum_tensor("PBIG", [128, 4096], F32))

        class BankView:
            def __init__(self, k):
                self.k = k

            def ap(self):
                return PBIG.ap()[:, 512 * self.k:512 * (self.k + 1)]

        pb = [BankView(i) for i in range(8)]

        for apv, sbuf, key in (
            (cin["ident"], IDENT, "IDENT"), (cin["had"], HAD, "HAD"),
            (cin["cos"], COS, "COS"), (cin["sin"], SIN, "SIN"),
            (cin["logp"], LOGP, "LOGP"), (cin["triu"], TRIU, "TRIU"),
            (cin["tril"], TRIL, "TRIL"), (cin["ceps"], CEPS, "CEPS"),
            (scal_in, SCAL, "SCAL"), (gn_in, GN, "GN"),
        ):
            em.dma("sync", "scst", sbuf.ap()[:, :], apv[:, :], writes=(key,))

        for key in ("IDENT", "HAD", "COS", "SIN", "LOGP", "TRIU", "TRIL",
                    "CEPS", "SCAL", "GN"):
            em.res[key]["w"]["scst"] = em.sems["scst"].n

        EPS6 = CEPS.ap()[:, 0:1]
        EPS5 = CEPS.ap()[:, 1:2]

        def col(c0, w=1):
            return ST.ap()[:, c0:c0 + w]

        def xnT_t(i, ti):
            return xnT.ap()[:, 2048 * i + 128 * ti:2048 * i + 128 * (ti + 1)]

        def xnT_c(i, c):
            return xnT.ap()[:, 2048 * i + 512 * c:2048 * i + 512 * (c + 1)]

        p0_pend = []

        def p0_transpose(ti, p):
            for bt in range(4):
                bk = 4 + bt % 2
                fns = [lambda e, o=pb[bk].ap()[:, 128 * k:128 * (k + 1)],
                       l=xnbf.ap()[:, p * D + 128 * (4 * bt + k):
                                   p * D + 128 * (4 * bt + k + 1)]:
                       e.matmul(o, l, IDENT.ap()[:, :], start=True, stop=True,
                                skip_group_check=True) for k in range(4)]
                em.group("tensor", fns, reads=(f"xnbf{p}", "IDENT"),
                         writes=(f"pb{bk}",))
                dst = xnT.ap()[:, :].rearrange("p (i t) -> p i t", t=T)[
                    :, 4 * bt:4 * (bt + 1), 128 * ti:128 * (ti + 1)]
                em.op("vector", lambda e, d=dst,
                      s=pb[bk].ap()[:, :].rearrange("p (i t) -> p i t",
                                                    t=128):
                      e.tensor_copy(d, s), reads=(f"pb{bk}",),
                      writes=("xnT",))

        for ti in range(NT):
            p = ti % 2
            xtile = xt.ap()[:, p * D:(p + 1) * D]
            kx = f"xt{p}"
            em.dma("sync", f"sx{p}", xtile, x_in[128 * ti:128 * (ti + 1), :],
                   writes=(kx,))
            s0 = 32 * p
            em.op("vector", lambda e, i_=xtile, o=col(s0 + 0):
                  e.reduce_sum(o, i_, axis=AX.X), reads=(kx,),
                  writes=(f"s1_{p}",))
            em.op("scalar", lambda e, i_=xtile, o=ln_scr.ap()[:, :],
                  acc=col(s0 + 1): e.activation(o, i_, AF.Square, accum_out=acc),
                  reads=(kx,), writes=("ln_scr", f"s2_{p}"))
            em.op("scalar", lambda e, i_=col(s0 + 0), o=col(s0 + 2):
                  e.mul(o, i_, 1.0 / D), reads=(f"s1_{p}",), writes=(f"m_{p}",))
            em.op("vector", lambda e, m=col(s0 + 2), o=col(s0 + 3):
                  e.tensor_mul(o, m, m), reads=(f"m_{p}",), writes=(f"msq_{p}",))
            em.op("vector", lambda e, s2=col(s0 + 1), msq=col(s0 + 3),
                  o=col(s0 + 4): e.tensor_scalar(o, s2, 1.0 / D, msq,
                                                 op0=ALU.mult, op1=ALU.subtract),
                  reads=(f"s2_{p}", f"msq_{p}"), writes=(f"var_{p}",))
            em.op("scalar", lambda e, v=col(s0 + 4), o=col(s0 + 5):
                  e.activation(o, v, AF.Ln, bias=EPS5),
                  reads=(f"var_{p}", "CEPS"), writes=(f"sd_{p}",))
            em.op("scalar", lambda e, i_=col(s0 + 5), o=col(s0 + 6):
                  e.activation(o, i_, AF.Exp, scale=-0.5),
                  reads=(f"sd_{p}",), writes=(f"rstd_{p}",))
            em.op("vector", lambda e, m=col(s0 + 2), r=col(s0 + 6),
                  o=col(s0 + 7): e.tensor_mul(o, m, r),
                  reads=(f"m_{p}", f"rstd_{p}"), writes=(f"mrs_{p}",))
            em.op("scalar", lambda e, i_=col(s0 + 7), o=col(s0 + 8):
                  e.mul(o, i_, -1.0), reads=(f"mrs_{p}",), writes=(f"nmrs_{p}",))
            xnb = xnbf.ap()[:, p * D:(p + 1) * D]
            em.op("vector", lambda e, i_=xtile, o=xnb, sc=col(s0 + 6),
                  bi=col(s0 + 8): e.tensor_scalar(o, i_, sc, bi, op0=ALU.mult,
                                                  op1=ALU.add),
                  reads=(kx, f"rstd_{p}", f"nmrs_{p}"), writes=(f"xnbf{p}",))
            if p0_pend:
                p0_pend.pop(0)()
            p0_pend.append(lambda ti=ti, p=p: p0_transpose(ti, p))
            continue
        _unused = None
        for _never in ():
            for bt in range(4):
                bk = 4 + bt % 2
                fns = [lambda e, o=pb[bk].ap()[:, 128 * k:128 * (k + 1)],
                       l=xnbf.ap()[:, p * D + 128 * (4 * bt + k):
                                   p * D + 128 * (4 * bt + k + 1)]:
                       e.matmul(o, l, IDENT.ap()[:, :], start=True, stop=True,
                                skip_group_check=True) for k in range(4)]
                em.group("tensor", fns, reads=(f"xnbf{p}", "IDENT"),
                         writes=(f"pb{bk}",))
                dst = xnT.ap()[:, :].rearrange("p (i t) -> p i t", t=T)[
                    :, 4 * bt:4 * (bt + 1), 128 * ti:128 * (ti + 1)]
                em.op("vector", lambda e, d=dst,
                      s=pb[bk].ap()[:, :].rearrange("p (i t) -> p i t", t=128):
                      e.tensor_copy(d, s), reads=(f"pb{bk}",), writes=("xnT",))

        while p0_pend:
            p0_pend.pop(0)()

        if "xnT" in dbg:
            dxnT = nc.dram_tensor("dbg_xnT", [128, ND * T], BF16,
                                  kind="ExternalOutput").ap()
            em.dma("gpsimd", "sdbg", dxnT[:, :], xnT.ap()[:, :],
                   reads=("xnT",))

        for i in range(ND):
            em.dma("gpsimd", "skv", wq_bf.ap()[:, 512 * i:512 * (i + 1)],
                   wk_in[128 * i:128 * (i + 1), :], writes=(f"wqbf{i}",))
        for i in range(ND):
            em.dma("gpsimd", "skv", gsil.ap()[:, 512 * i:512 * (i + 1)],
                   wv_in[128 * i:128 * (i + 1), :], writes=(f"wvbf{i}",))
        for i in range(ND):
            em.res[f"wqbf{i}"]["w"]["skv"] = em.sems["skv"].n
            em.res[f"wvbf{i}"]["w"]["skv"] = em.sems["skv"].n

        em.barrier(extra_streams=())
        em.res["xnT"] = {"w": {"pg_vector": em.prog["vector"].n}, "r": {}}
        for i in range(ND):
            em.res[f"wqbf{i}"] = {"w": {"skv": 16 * 2 * ND}, "r": {}}
            em.res[f"wvbf{i}"] = {"w": {"skv": 16 * 2 * ND}, "r": {}}
        for key in ("IDENT", "HAD", "COS", "SIN", "LOGP", "TRIU", "TRIL",
                    "CEPS", "SCAL", "GN"):
            em.res[key] = {"w": {"scst": em.sems["scst"].n}, "r": {}}

        def rope_emit(srcv, dstv, ti, rkeys, wkeys):
            cosv = COS.ap()[:, 64 * ti:64 * (ti + 1)].unsqueeze(1).broadcast_to(
                (128, 4, 64))
            sinv = SIN.ap()[:, 64 * ti:64 * (ti + 1)].unsqueeze(1).broadcast_to(
                (128, 4, 64))
            x1 = srcv[:, :, 0:64]
            x2 = srcv[:, :, 64:128]
            y1 = dstv[:, :, 0:64]
            y2 = dstv[:, :, 64:128]
            t1 = rtmp.ap()[:, 0:256].rearrange("p (h e) -> p h e", e=64)
            t2 = rtmp.ap()[:, 256:512].rearrange("p (h e) -> p h e", e=64)
            em.op("vector", lambda e: e.tensor_mul(t1, x1, cosv),
                  reads=rkeys + ("COS",), writes=("rt1",))
            em.op("vector", lambda e: e.tensor_mul(t2, x2, sinv),
                  reads=rkeys + ("SIN",), writes=("rt2",))
            em.op("vector", lambda e: e.tensor_add(y1, t1, t2),
                  reads=("rt1", "rt2"), writes=wkeys)
            em.op("vector", lambda e: e.tensor_mul(t1, x2, cosv),
                  reads=rkeys + ("COS",), writes=("rt1",))
            em.op("vector", lambda e: e.tensor_mul(t2, x1, sinv),
                  reads=rkeys + ("SIN",), writes=("rt2",))
            em.op("vector", lambda e: e.tensor_sub(y2, t1, t2),
                  reads=("rt1", "rt2"), writes=wkeys)

        def proj_group(bank, bkey, ti, rhs_handle, rkeys):
            fns = [lambda e, o=pb[bank].ap()[:, :], l=xnT_t(i, ti),
                   r=rhs_handle.ap()[:, 512 * i:512 * (i + 1)],
                   s=(i == 0), st=(i == ND - 1):
                   e.matmul(o, l, r, start=s, stop=st, skip_group_check=True)
                   for i in range(ND)]
            em.group("tensor", fns, reads=rkeys + ("xnT",), writes=(bkey,))

        def rms4(bank_ap, bkey, p, out_keybase):
            s0 = 32 * p
            for h in range(4):
                em.op("scalar", lambda e,
                      o=sqs.ap()[:, 512 * p + 128 * h:512 * p + 128 * (h + 1)],
                      i_=bank_ap[:, 128 * h:128 * (h + 1)],
                      acc=col(s0 + 9 + h):
                      e.activation(o, i_, AF.Square, accum_out=acc),
                      reads=(bkey,), writes=(f"sqs{p}", f"kss{p}"))
            em.op("scalar", lambda e, i_=col(s0 + 9, 4), o=col(s0 + 13, 4):
                  e.activation(o, i_, AF.Ln, scale=1.0 / DH, bias=EPS6),
                  reads=(f"kss{p}", "CEPS"), writes=(f"ksd{p}",))
            em.op("scalar", lambda e, i_=col(s0 + 13, 4), o=col(s0 + 17, 4):
                  e.activation(o, i_, AF.Exp, scale=-0.5),
                  reads=(f"ksd{p}",), writes=(out_keybase,))

        def emit_kq_transpose(ti, dst_handle, dst_key):
            p = ti % 2
            tb = 4 + p
            fns = [lambda e, o=pb[tb].ap()[:, 128 * h:128 * (h + 1)],
                   l=kqbf.ap()[:, 512 * p + 128 * h:512 * p + 128 * (h + 1)]:
                   e.matmul(o, l, IDENT.ap()[:, :], start=True, stop=True,
                            skip_group_check=True) for h in range(4)]
            em.group("tensor", fns, reads=(f"kqbf{p}", "IDENT"),
                     writes=(f"pb{tb}",))
            dst = dst_handle.ap()[:, :].rearrange("p (g t) -> p g t", t=T)[
                :, :, 128 * ti:128 * (ti + 1)]
            em.op("vector", lambda e, d=dst,
                  s=pb[tb].ap()[:, :].rearrange("p (g t) -> p g t", t=128):
                  e.tensor_copy(d, s), reads=(f"pb{tb}",), writes=(dst_key,))

        pend_t = []
        for ti in range(NT):
            p = ti % 2
            kb, kkey = p, f"pb{p}"
            vb, vkey = 2 + p, f"pb{2 + p}"
            proj_group(kb, kkey, ti, wq_bf, tuple(f"wqbf{i}" for i in range(ND)))
            proj_group(vb, vkey, ti, gsil, tuple(f"wvbf{i}" for i in range(ND)))
            if pend_t:
                pend_t.pop(0)()
            s0 = 32 * p
            rms4(pb[kb].ap(), kkey, p, f"krs{p}")
            kn = qf.ap()[:, 512 * p:512 * (p + 1)].rearrange(
                "p (h e) -> p h e", e=128)
            em.op("vector", lambda e, o=kn,
                  i_=pb[kb].ap()[:, :].rearrange("p (h e) -> p h e", e=128),
                  r=col(s0 + 17, 4).unsqueeze(2).broadcast_to((128, 4, 128)):
                  e.tensor_tensor(o, i_, r, op=ALU.mult),
                  reads=(kkey, f"krs{p}"), writes=(f"qf{p}",))
            kro = sqs.ap()[:, 512 * p:512 * (p + 1)].rearrange(
                "p (h e) -> p h e", e=128)
            rope_emit(kn, kro, ti, (f"qf{p}",), (f"sqs{p}",))
            em.op("scalar", lambda e, o=kqbf.ap()[:, 512 * p:512 * (p + 1)],
                  i_=sqs.ap()[:, 512 * p:512 * (p + 1)]: e.copy(o, i_),
                  reads=(f"sqs{p}",), writes=(f"kqbf{p}",))
            pend_t.append(lambda ti=ti: emit_kq_transpose(ti, kT, "kT"))
            em.op("scalar", lambda e, o=v_st.ap()[:, 512 * ti:512 * (ti + 1)],
                  i_=pb[vb].ap()[:, :]: e.copy(o, i_),
                  reads=(vkey,), writes=("v_st",))
        while pend_t:
            pend_t.pop(0)()

        if "kv" in dbg:
            dkT = nc.dram_tensor("dbg_kT", [128, GK * T], BF16,
                                 kind="ExternalOutput").ap()
            em.dma("gpsimd", "sdbg", dkT[:, :], kT.ap()[:, :], reads=("kT",))
            dv = nc.dram_tensor("dbg_v", [128, NT * 512], BF16,
                                kind="ExternalOutput").ap()
            em.dma("gpsimd", "sdbg", dv[:, :], v_st.ap()[:, :], reads=("v_st",))

        for ti in range(NT):
            em.overlay(f"gsil{ti}", [f"wvbf{ti}"])

        for g in range(GK):
            for i in range(ND):
                em.dma("gpsimd", "swq", wq_bf.ap()[:, 512 * i:512 * (i + 1)],
                       wq_in[128 * i:128 * (i + 1), 512 * g:512 * (g + 1)],
                       writes=(f"wqbf{i}",))
            for i in range(ND):
                em.res[f"wqbf{i}"]["w"]["swq"] = em.sems["swq"].n
            pend_t = []
            for ti in range(NT):
                p = ti % 2
                qb, qkey = p, f"pb{p}"
                proj_group(qb, qkey, ti, wq_bf,
                           tuple(f"wqbf{i}" for i in range(ND)))
                if pend_t:
                    pend_t.pop(0)()
                s0 = 32 * p
                rms4(pb[qb].ap(), qkey, p, f"krs{p}")
                em.op("vector", lambda e, o=col(s0 + 21, 4),
                      i_=SCAL.ap()[:, 4 * g:4 * (g + 1)],
                      s=LOGP.ap()[:, ti:ti + 1]:
                      e.tensor_scalar(o, i_, s, None, op0=ALU.mult),
                      reads=("SCAL", "LOGP"), writes=(f"lsc{p}",))
                em.op("vector", lambda e, a_=col(s0 + 17, 4),
                      b_=col(s0 + 21, 4), o=col(s0 + 25, 4):
                      e.tensor_mul(o, a_, b_),
                      reads=(f"krs{p}", f"lsc{p}"), writes=(f"qrall{p}",))
                qn = qf.ap()[:, 512 * p:512 * (p + 1)].rearrange(
                    "p (h e) -> p h e", e=128)
                em.op("vector", lambda e, o=qn,
                      i_=pb[qb].ap()[:, :].rearrange("p (h e) -> p h e", e=128),
                      r=col(s0 + 25, 4).unsqueeze(2).broadcast_to((128, 4, 128)):
                      e.tensor_tensor(o, i_, r, op=ALU.mult),
                      reads=(qkey, f"qrall{p}"), writes=(f"qf{p}",))
                qro = sqs.ap()[:, 512 * p:512 * (p + 1)].rearrange(
                    "p (h e) -> p h e", e=128)
                rope_emit(qn, qro, ti, (f"qf{p}",), (f"sqs{p}",))
                em.op("scalar", lambda e, o=kqbf.ap()[:, 512 * p:512 * (p + 1)],
                      i_=sqs.ap()[:, 512 * p:512 * (p + 1)]: e.copy(o, i_),
                      reads=(f"sqs{p}",), writes=(f"kqbf{p}",))
                pend_t.append(lambda ti=ti: emit_kq_transpose(ti, qT, "qT"))
            while pend_t:
                pend_t.pop(0)()
            for i in range(ND):
                em.dma("gpsimd", "swq", wq_bf.ap()[:, 512 * i:512 * (i + 1)],
                       wg_in[128 * i:128 * (i + 1), 512 * g:512 * (g + 1)],
                       writes=(f"wqbf{i}",))
            for i in range(ND):
                em.res[f"wqbf{i}"]["w"]["swq"] = em.sems["swq"].n
            for ti in range(NT):
                p = ti % 2
                gb, gkey = 2 + p, f"pb{2 + p}"
                proj_group(gb, gkey, ti, wq_bf,
                           tuple(f"wqbf{i}" for i in range(ND)))
                em.op("scalar", lambda e,
                      o=gsil.ap()[:, 512 * ti:512 * (ti + 1)],
                      i_=pb[gb].ap()[:, :]: e.activation(o, i_, AF.Silu),
                      reads=(gkey,), writes=(f"gsil{ti}",))
            def unit_params(u):
                ti, h = u // 4, u % 4
                Wt = min(ti, 8) + 1
                jlo = max(0, ti - 8)
                win = 128 * Wt
                nch = (win + 511) // 512
                hp = h % 2
                return ti, h, Wt, jlo, win, nch, hp

            def emitA(u):
                ti, h, Wt, jlo, win, nch, hp = unit_params(u)
                sbanks = [3 * hp + c for c in range(nch)]
                span0 = 512 * 3 * hp
                fns = []
                for c in range(nch):
                    c0 = 512 * c
                    w = min(512, win - c0)
                    fns.append(
                        lambda e, o=pb[sbanks[c]].ap()[:, 0:w],
                        l=qT.ap()[:, 2048 * h + 128 * ti:
                                  2048 * h + 128 * ti + 128],
                        r=kT.ap()[:, 2048 * g + 128 * jlo + c0:
                                  2048 * g + 128 * jlo + c0 + w]:
                        e.matmul(o, l, r, start=True, stop=False,
                                 skip_group_check=True))
                if ti >= 8:
                    fns.append(lambda e, o=PBIG.ap()[:, span0:span0 + 128]:
                               e.matmul(o, IDENT.ap()[:, :], TRIU.ap()[:, :],
                                        start=False, stop=False,
                                        skip_group_check=True))
                fns.append(lambda e,
                           o=PBIG.ap()[:, span0 + win - 128:span0 + win]:
                           e.matmul(o, IDENT.ap()[:, :], TRIL.ap()[:, :],
                                    start=False, stop=True,
                                    skip_group_check=True))
                em.group("tensor", fns, reads=("qT", "kT", "IDENT", "TRIU",
                                               "TRIL"),
                         writes=tuple(f"pb{b_}" for b_ in sbanks))

            def emitB(u):
                ti, h, Wt, jlo, win, nch, hp = unit_params(u)
                sbanks = [3 * hp + c for c in range(nch)]
                skeys = tuple(f"pb{b_}" for b_ in sbanks)
                span = PBIG.ap()[:, 512 * 3 * hp:512 * 3 * hp + win]
                s0 = 64 + 16 * hp
                em.op("vector", lambda e, o=col(s0 + 3), i_=span:
                      e.reduce_max(o, i_, axis=AX.X, negate=True),
                      reads=skeys, writes=(f"nm{hp}",))
                em.op("scalar", lambda e,
                      o=p_bf.ap()[:, 1280 * hp:1280 * hp + win],
                      i_=span, acc=col(s0 + 8), bi=col(s0 + 3):
                      e.activation(o, i_, AF.Exp, bias=bi,
                                   accum_out=acc),
                      reads=skeys + (f"nm{hp}",),
                      writes=(f"p_bf{hp}", f"p_bf{hp}b", f"l{hp}"))
                em.op("vector", lambda e, o=col(s0 + 9 + h),
                      i_=col(s0 + 8): e.reciprocal(o, i_),
                      reads=(f"l{hp}",), writes=(f"rl_{h}",))

            def emitC(u):
                ti, h, Wt, jlo, win, nch, hp = unit_params(u)
                ykey = f"pb{6 + hp}"
                nb = (Wt + 3) // 4

                def pT_batch(bt):
                    ntile = min(4, Wt - 4 * bt)
                    tbk = 3 * hp + (bt % 3)
                    fns = [lambda e,
                           o=pb[tbk].ap()[:, 128 * k:128 * (k + 1)],
                           l=p_bf.ap()[:, 1280 * hp + 128 * (4 * bt + k):
                                       1280 * hp + 128 * (4 * bt + k + 1)]:
                           e.matmul(o, l, IDENT.ap()[:, :], start=True,
                                    stop=True, skip_group_check=True)
                           for k in range(ntile)]
                    pkeys = ((f"p_bf{hp}",) if 4 * bt + ntile <= 4
                             else (f"p_bf{hp}", f"p_bf{hp}b"))
                    em.group("tensor", fns, reads=pkeys + ("IDENT",),
                             writes=(f"pb{tbk}",))

                def evac_batch(bt):
                    ntile = min(4, Wt - 4 * bt)
                    tbk = 3 * hp + (bt % 3)
                    pt = bt % 2
                    eng = "scalar" if bt % 2 == 0 else "vector"
                    if eng == "scalar":
                        em.op("scalar", lambda e,
                              o=pT_bf.ap()[:, 512 * pt:512 * pt + 128 * ntile],
                              i_=pb[tbk].ap()[:, 0:128 * ntile]:
                              e.copy(o, i_), reads=(f"pb{tbk}",),
                              writes=(f"pT{pt}",))
                    else:
                        em.op("vector", lambda e,
                              o=pT_bf.ap()[:, 512 * pt:512 * pt + 128 * ntile],
                              i_=pb[tbk].ap()[:, 0:128 * ntile]:
                              e.tensor_copy(o, i_), reads=(f"pb{tbk}",),
                              writes=(f"pT{pt}",))

                def pv_batch(bt):
                    ntile = min(4, Wt - 4 * bt)
                    pt = bt % 2
                    fns = [lambda e, o=pb[6 + hp].ap()[:, 0:128],
                           l=pT_bf.ap()[:, 512 * pt + 128 * k:
                                        512 * pt + 128 * (k + 1)],
                           r=v_st.ap()[:, 512 * (jlo + 4 * bt + k) + 128 * g:
                                       512 * (jlo + 4 * bt + k) + 128 * (g + 1)],
                           s=(bt == 0 and k == 0),
                           st=(4 * bt + k == Wt - 1):
                           e.matmul(o, l, r, start=s, stop=st,
                                    skip_group_check=True)
                           for k in range(ntile)]
                    em.group("tensor", fns, reads=(f"pT{pt}", "v_st"),
                             writes=(ykey,))

                pT_batch(0)
                for bt in range(nb):
                    if bt + 1 < nb:
                        pT_batch(bt + 1)
                    evac_batch(bt)
                    pv_batch(bt)

            def emitD(u):
                ti, h, Wt, jlo, win, nch, hp = unit_params(u)
                ykey = f"pb{6 + hp}"
                em.op("vector", lambda e,
                      o=w_all.ap()[:, 128 * h:128 * (h + 1)],
                      i0=pb[6 + hp].ap()[:, 0:128],
                      sc=col(64 + 16 * hp + 9 + h),
                      i1=gsil.ap()[:, 512 * ti + 128 * h:
                                   512 * ti + 128 * (h + 1)]:
                      e.scalar_tensor_tensor(o, i0, sc, i1, op0=ALU.mult,
                                             op1=ALU.mult),
                      reads=(ykey, f"rl_{h}", f"gsil{ti}"),
                      writes=("w_all",))
                em.op("scalar", lambda e,
                      o=sqs.ap()[:, 128 * h:128 * (h + 1)],
                      i_=w_all.ap()[:, 128 * h:128 * (h + 1)],
                      acc=col(96 + h): e.activation(o, i_, AF.Square,
                                                    accum_out=acc),
                      reads=("w_all",), writes=("sqs0", "wss"))
                if h != 3:
                    return
                em.op("scalar", lambda e, i_=col(96, 4), o=col(100, 4):
                      e.activation(o, i_, AF.Ln, scale=1.0 / DH, bias=EPS6),
                      reads=("wss", "CEPS"), writes=("wsd",))
                em.op("scalar", lambda e, i_=col(100, 4), o=col(104, 4):
                      e.activation(o, i_, AF.Exp, scale=-0.5),
                      reads=("wsd",), writes=("rw",))
                em.op("vector", lambda e,
                      o=w_all2.ap()[:, :].rearrange("p (h e) -> p h e", e=128),
                      i_=w_all.ap()[:, :].rearrange("p (h e) -> p h e", e=128),
                      r=col(104, 4).unsqueeze(2).broadcast_to((128, 4, 128)):
                      e.tensor_tensor(o, i_, r, op=ALU.mult),
                      reads=("w_all", "rw"), writes=("w_all2",))
                sp = ti % 2
                em.op("vector", lambda e,
                      o=sstage.ap()[:, 512 * sp:512 * (sp + 1)].rearrange(
                          "p (h e) -> p h e", e=128),
                      i_=w_all2.ap()[:, :].rearrange("p (h e) -> p h e", e=128),
                      r=GN.ap()[:, :].unsqueeze(1).broadcast_to((128, 4, 128)):
                      e.tensor_tensor(o, i_, r, op=ALU.mult),
                      reads=("w_all2", "GN"), writes=(f"sstage{sp}",))
                em.dma("gpsimd", "s_sout",
                       s_scr[128 * ti:128 * (ti + 1), 512 * g:512 * (g + 1)],
                       sstage.ap()[:, 512 * sp:512 * (sp + 1)],
                       reads=(f"sstage{sp}",), writes=("s_scr",))

            NU = NT * 4
            emitA(0)
            emitB(0)
            for u in range(1, NU):
                emitA(u)
                emitC(u - 1)
                emitB(u)
                emitD(u - 1)
            emitC(NU - 1)
            emitD(NU - 1)

        if "qT" in dbg:
            dqT = nc.dram_tensor("dbg_qT", [128, 4 * T], BF16,
                                 kind="ExternalOutput").ap()
            em.dma("gpsimd", "sdbg", dqT[:, :], qT.ap()[:, :], reads=("qT",))
            dgs = nc.dram_tensor("dbg_gsil", [128, NT * 512], BF16,
                                 kind="ExternalOutput").ap()
            em.dma("gpsimd", "sdbg", dgs[:, :], gsil.ap()[:, :],
                   reads=tuple(f"gsil{t}" for t in range(NT)))

        em.barrier(extra_streams=("s_sout",))
        em.res["s_scr"] = {"w": {"s_sout": em.sems["s_sout"].n}, "r": {}}
        for key in ("IDENT", "HAD", "COS", "SIN", "LOGP", "TRIU", "TRIL",
                    "CEPS", "SCAL", "GN", "xnT"):
            em.res[key] = {"w": {}, "r": {}}

        w2all = []

        def emit_w2pre_load_and_bfly(dt):
            src = w2x if dt % 2 == 0 else w2y
            skey = "w2x" if dt % 2 == 0 else "w2y"
            scr, ckey = w2z, "w2z"
            w2all.append(lambda dt=dt, src=src, skey=skey: em.dma(
                "gpsimd", f"sw2_{dt % 2}", src.ap()[:, :],
                w2_in[128 * dt:128 * (dt + 1), :], writes=(skey,)))
            for s in range(6):
                blk = 128 * (1 << s)
                fr, to = (src, scr) if s % 2 == 0 else (scr, src)
                frk, tok = (skey, ckey) if s % 2 == 0 else (ckey, skey)
                frv = fr.ap()[:, :].rearrange("p (n two b) -> p n two b",
                                              two=2, b=blk)
                tov = to.ap()[:, :].rearrange("p (n two b) -> p n two b",
                                              two=2, b=blk)
                w2all.append(lambda frv=frv, tov=tov, frk=frk, tok=tok:
                             em.op("vector", lambda e, o=tov[:, :, 0, :],
                                   a_=frv[:, :, 0, :], b_=frv[:, :, 1, :]:
                                   e.tensor_add(o, a_, b_),
                                   reads=(frk,), writes=(tok,)))
                w2all.append(lambda frv=frv, tov=tov, frk=frk, tok=tok:
                             em.op("vector", lambda e, o=tov[:, :, 1, :],
                                   a_=frv[:, :, 0, :], b_=frv[:, :, 1, :]:
                                   e.tensor_sub(o, a_, b_),
                                   reads=(frk,), writes=(tok,)))
            for bt in range(16):
                w2all.append(lambda dt=dt, bt=bt, src=src, skey=skey:
                             emit_one_w2batch(dt, bt, src, skey))

        w2h_pend = []

        def emit_w2pre_h(item):
            dt, bt = item
            em.group("tensor",
                     [lambda e, o=pb[7].ap()[:, :], l=HAD.ap()[:, :],
                      r=w2st.ap()[:, 512 * bt:512 * (bt + 1)]:
                      e.matmul(o, l, r, start=True, stop=True,
                               skip_group_check=True)],
                     reads=(f"w2st{bt % 2}", "HAD"), writes=("pb7",))
            em.op("scalar", lambda e, o=w2st.ap()[:, 512 * bt:512 * (bt + 1)],
                  i_=pb[7].ap()[:, :]: e.activation(o, i_, AF.Copy,
                                                    scale=FWHT_SCALE),
                  reads=("pb7",), writes=(f"w2st{bt % 2}",))
            if bt == 15:
                dstv = w2pp[dt // 2].rearrange("p (a e) -> p a e", e=256)[
                    :, :, 128 * (dt % 2):128 * (dt % 2) + 128]
                srcv = w2st.ap()[:, :].rearrange("p (a e) -> p a e", e=128)
                em.dma("gpsimd", "sw2pp", dstv[:, :, :], srcv[:, :, :],
                       reads=("w2st0", "w2st1"), writes=("w2pp",))

        def emit_one_w2batch(dt, bt, src, skey):
            fns = [lambda e, o=pb[6].ap()[:, 128 * k:128 * (k + 1)],
                   l=src.ap()[:, 128 * (4 * bt + k):128 * (4 * bt + k + 1)]:
                   e.matmul(o, l, IDENT.ap()[:, :], start=True, stop=True,
                            skip_group_check=True) for k in range(4)]
            em.group("tensor", fns, reads=(skey, "IDENT"), writes=("pb6",))
            em.op("vector", lambda e, o=w2st.ap()[:, 512 * bt:512 * (bt + 1)],
                  i_=pb[6].ap()[:, :]: e.tensor_copy(o, i_),
                  reads=("pb6",), writes=(f"w2st{bt % 2}",))
            if w2h_pend:
                emit_w2pre_h(w2h_pend.pop(0))
            w2h_pend.append((dt, bt))
            if bt == 15:
                while w2h_pend:
                    emit_w2pre_h(w2h_pend.pop(0))

        def emit_wt(aa, nat, matT, natkey, tkey, bank, batch):
            sa_, pa = aa % 4, aa % 2
            fns = [lambda e, o=pb[bank].ap()[:, 128 * k:128 * (k + 1)],
                   l=nat.ap()[:, sa_ * D + 128 * (4 * batch + k):
                              sa_ * D + 128 * (4 * batch + k + 1)]:
                   e.matmul(o, l, IDENT.ap()[:, :], start=True, stop=True,
                            skip_group_check=True) for k in range(4)]
            em.group("tensor", fns, reads=(f"{natkey}{sa_}", "IDENT"),
                     writes=(f"pb{bank}",))
            em.op("scalar", lambda e,
                  o=matT.ap()[:, pa * D + 512 * batch:pa * D + 512 * (batch + 1)],
                  i_=pb[bank].ap()[:, :]: e.copy(o, i_),
                  reads=(f"pb{bank}",), writes=(f"{tkey}{pa}",))

        def load_w13(aa):
            sa_ = aa % 4
            em.dma("gpsimd", f"sw1_{sa_}", w1nat.ap()[:, sa_ * D:(sa_ + 1) * D],
                   w1_in[128 * aa:128 * (aa + 1), :], writes=(f"w1nat{sa_}",))
            em.dma("gpsimd", f"sw3_{sa_}", w3nat.ap()[:, sa_ * D:(sa_ + 1) * D],
                   w3_in[128 * aa:128 * (aa + 1), :], writes=(f"w3nat{sa_}",))

        load_w13(0)
        load_w13(1)
        for aa in range(NA):
            pa = aa % 2
            if aa + 2 < NA:
                load_w13(aa + 2)
            if aa % 3 == 0 and aa // 3 < ND:
                emit_w2pre_load_and_bfly(aa // 3)
            if aa == 0:
                for bt in range(4):
                    emit_wt(0, w1nat, w1T, "w1nat", "w1T", 6, bt)
                    emit_wt(0, w3nat, w3T, "w3nat", "w3T", 7, bt)
            for c in range(4):
                ub, ukey = c % 2, f"pb{c % 2}"
                fns = [lambda e, o=pb[ub].ap()[:, :],
                       l=w1T.ap()[:, pa * D + 128 * i:pa * D + 128 * (i + 1)],
                       r=xnT_c(i, c), s=(i == 0), st=(i == ND - 1):
                       e.matmul(o, l, r, start=s, stop=st,
                                skip_group_check=True) for i in range(ND)]
                em.group("tensor", fns, reads=(f"w1T{pa}", "xnT"),
                         writes=(ukey,))
                em.op("scalar", lambda e,
                      o=sg_scr.ap()[:, 512 * (c % 2):512 * (c % 2 + 1)],
                      i_=pb[ub].ap()[:, :]: e.activation(o, i_, AF.Silu),
                      reads=(ukey,), writes=(f"sg{c % 2}",))
                if aa + 1 < NA:
                    emit_wt(aa + 1, w1nat, w1T, "w1nat", "w1T", 6, c)
                lb, lkey = 2 + c % 2, f"pb{2 + c % 2}"
                fns = [lambda e, o=pb[lb].ap()[:, :],
                       l=w3T.ap()[:, pa * D + 128 * i:pa * D + 128 * (i + 1)],
                       r=xnT_c(i, c), s=(i == 0), st=(i == ND - 1):
                       e.matmul(o, l, r, start=s, stop=st,
                                skip_group_check=True) for i in range(ND)]
                em.group("tensor", fns, reads=(f"w3T{pa}", "xnT"),
                         writes=(lkey,))
                em.op("vector", lambda e,
                      o=hidb.ap()[:, pa * D + 512 * c:pa * D + 512 * (c + 1)],
                      a_=sg_scr.ap()[:, 512 * (c % 2):512 * (c % 2 + 1)],
                      b_=pb[lb].ap()[:, :]: e.tensor_mul(o, a_, b_),
                      reads=(f"sg{c % 2}", lkey), writes=(f"hidb{pa}",))
                if aa + 1 < NA:
                    emit_wt(aa + 1, w3nat, w3T, "w3nat", "w3T", 7, c)
                em.dma("gpsimd", "shid_out", hid_scr[c, aa][:, :],
                       hidb.ap()[:, pa * D + 512 * c:pa * D + 512 * (c + 1)],
                       reads=(f"hidb{pa}",), writes=("hid_scr",))
                for _ in range(2):
                    if w2all:
                        w2all.pop(0)()
        while w2all:
            w2all.pop(0)()
        while w2h_pend:
            emit_w2pre_h(w2h_pend.pop(0))

        em.barrier(extra_streams=("shid_out", "sw2pp", "s_sout"))
        for nm, src, shape in (("s", s_scr, [NT * 128, D]),
                               ("hid", hid_scr, [4, NA, 128, 512])):
            if nm in dbg:
                dd = nc.dram_tensor(f"dbg_{nm}", shape, BF16,
                                    kind="ExternalOutput").ap()
                key = {"s": "s_scr", "hid": "hid_scr", "w2pp": "w2pp"}[nm]
                em.dma("gpsimd", "sdbg", dd[:, :] if len(shape) == 2
                       else dd[:, :, :, :], src[:, :] if len(shape) == 2
                       else src[:, :, :, :], reads=(key,))
        em.res["hid_scr"] = {"w": {"shid_out": em.sems["shid_out"].n}, "r": {}}
        em.res["w2pp"] = {"w": {"sw2pp": em.sems["sw2pp"].n}, "r": {}}
        em.res["s_scr"] = {"w": {"s_sout": em.sems["s_sout"].n}, "r": {}}

        for c in range(4):
            for dc0 in range(2):
                w2cb0 = w2c0 if dc0 == 0 else w2c1
                for sl in range(4):
                    seg = HF // 2
                    em.dma("sync", f"sw2c{dc0}",
                           w2cb0.ap()[:, seg * sl:seg * (sl + 1)],
                           w2pp[dc0][:, seg * sl:seg * (sl + 1)],
                           reads=("w2pp",), writes=(f"w2cb{dc0}",))
            for q in range(4):
                em.dma("sync", f"shid_in{q}",
                       hidc.ap()[:, :].rearrange("p (a e) -> p a e", e=512)[
                           :, 16 * q:16 * (q + 1), :],
                       hid_scr[c, 16 * q:16 * (q + 1)].transpose([1, 0, 2]),
                       reads=("hid_scr",), writes=(f"hidcQ{q}",))
            for t1 in range(4):
                ti = 4 * c + t1
                em.dma("sync", f"sxr{t1}",
                       outstage.ap()[:, D * t1:D * (t1 + 1)],
                       x_in[128 * ti:128 * (ti + 1), :], writes=(f"os{t1}",))
            for dt in range(ND):
                em.dma("sync", "ss_in",
                       sTb.ap()[:, 512 * dt:512 * (dt + 1)],
                       s_scr[512 * c:512 * (c + 1),
                             128 * dt:128 * (dt + 1)],
                       reads=("s_scr",), writes=("sTb",), transpose=True)
            for dc in range(8):
                pd = dc % 2
                w2cb = w2c0 if pd == 0 else w2c1
                w2key = f"w2cb{pd}"
                if dc >= 2:
                    for sl in range(4):
                        seg = HF // 2
                        em.dma("sync", f"sw2c{pd}",
                               w2cb.ap()[:, seg * sl:seg * (sl + 1)],
                               w2pp[dc][:, seg * sl:seg * (sl + 1)],
                               reads=("w2pp",), writes=(w2key,))
                for db in range(2):
                    dt_g = 2 * dc + db
                    bank = 2 * pd + db
                    bkey = f"pb{bank}"
                    for q in range(4):
                        fns = [lambda e, o=pb[bank].ap()[:, :],
                               l=w2cb.ap()[:, 256 * a2 + 128 * db:
                                           256 * a2 + 128 * (db + 1)],
                               r=hidc.ap()[:, 512 * a2:512 * (a2 + 1)],
                               s=(a2 == 0), st=(a2 == NA - 1):
                               e.matmul(o, l, r, start=s, stop=st,
                                        skip_group_check=True)
                               for a2 in range(16 * q, 16 * (q + 1))]
                        em.group("tensor", fns,
                                 reads=(f"hidcQ{q}", w2key), writes=(bkey,))
                    gp = dt_g % 2
                    em.op("scalar", lambda e,
                          o=gfb.ap()[:, 512 * gp:512 * (gp + 1)],
                          i_=pb[bank].ap()[:, :]:
                          e.activation(o, i_, AF.Silu),
                          reads=(bkey,), writes=(f"gf{gp}",))
                    em.op("vector", lambda e,
                          o=wbf.ap()[:, 512 * gp:512 * (gp + 1)],
                          a_=gfb.ap()[:, 512 * gp:512 * (gp + 1)],
                          b_=sTb.ap()[:, 512 * dt_g:512 * (dt_g + 1)]:
                          e.tensor_mul(o, a_, b_),
                          reads=(f"gf{gp}", "sTb"), writes=(f"wbf{gp}",))
                    tbank = 4 + gp
                    fns = [lambda e,
                           o=pb[tbank].ap()[:, 128 * k:128 * (k + 1)],
                           l=wbf.ap()[:, 512 * gp + 128 * k:
                                      512 * gp + 128 * (k + 1)]:
                           e.matmul(o, l, IDENT.ap()[:, :], start=True,
                                    stop=True, skip_group_check=True)
                           for k in range(4)]
                    em.group("tensor", fns, reads=(f"wbf{gp}", "IDENT"),
                             writes=(f"pb{tbank}",))
                    for k in range(4):
                        em.op("vector", lambda e,
                              o=outstage.ap()[:, D * k + 128 * dt_g:
                                              D * k + 128 * (dt_g + 1)],
                              i_=pb[tbank].ap()[:, 128 * k:128 * (k + 1)]:
                              e.tensor_add(o, o, i_),
                              reads=(f"pb{tbank}", f"os{k}"),
                              writes=(f"os{k}",))
            for t1 in range(4):
                ti = 4 * c + t1
                em.dma("scalar", f"sout{t1}",
                       out_d[128 * ti:128 * (ti + 1), :],
                       outstage.ap()[:, D * t1:D * (t1 + 1)],
                       reads=(f"os{t1}",))

        for t1 in range(4):
            em.wait("sync", f"sout{t1}", em.sems[f"sout{t1}"].n)

        with nc.Block() as block:
            @block.sync
            def _(e):
                for f in em.ops["sync"]:
                    f(e)

            @block.gpsimd
            def _(e):
                for f in em.ops["gpsimd"]:
                    f(e)

            @block.vector
            def _(e):
                for f in em.ops["vector"]:
                    f(e)

            @block.scalar
            def _(e):
                for f in em.ops["scalar"]:
                    f(e)

            @block.tensor
            def _(e):
                for f in em.ops["tensor"]:
                    f(e)

    return nc


def kernel(L0_attn_hidden_states, L0_attn_Wq, L0_attn_Wk, L0_attn_Wv,
           L0_attn_Wg, L0_attn_softmax_scaler, L0_attn_gn_weight,
           L0_ffn_W1, L0_ffn_W2, L0_ffn_W3):
    x = np.ascontiguousarray(np.asarray(L0_attn_hidden_states, np.float32))
    wq = np.ascontiguousarray(np.asarray(L0_attn_Wq, np.float32))
    wk = np.ascontiguousarray(np.asarray(L0_attn_Wk, np.float32))
    wv = np.ascontiguousarray(np.asarray(L0_attn_Wv, np.float32))
    wg = np.ascontiguousarray(np.asarray(L0_attn_Wg, np.float32))
    w1 = np.ascontiguousarray(np.asarray(L0_ffn_W1, np.float32))
    w2 = np.ascontiguousarray(np.asarray(L0_ffn_W2, np.float32))
    w3 = np.ascontiguousarray(np.asarray(L0_ffn_W3, np.float32))
    scal = np.ascontiguousarray(np.broadcast_to(
        np.asarray(L0_attn_softmax_scaler, np.float32)[None, :], (128, H)))
    gn = np.ascontiguousarray(np.broadcast_to(
        np.asarray(L0_attn_gn_weight, np.float32)[None, :], (128, DH)))

    nc = build_program()
    in_maps = [{
        "x": np.ascontiguousarray(x[b]),
        "wq": wq, "wk": wk, "wv": wv, "wg": wg,
        "scaler_rep": scal, "gn_rep": gn,
        "w1": w1, "w2": w2, "w3": w3,
    } for b in range(8)]
    res = run_bass_kernel_spmd(nc, in_maps, core_ids=list(range(8)))
    out = np.stack([res.results[b]["out"] for b in range(8)], axis=0)
    return out.astype(np.float32)


if __name__ == "__main__":
    nc = build_program()
    print("build ok; instructions:", nc.n_instructions())
